# version 26
# baseline (speedup 1.0000x reference)
import numpy as np
import ml_dtypes
import concourse.bass as bass
import concourse.mybir as mybir
from concourse.bass_utils import run_bass_kernel_spmd

F32 = mybir.dt.float32
BF16 = mybir.dt.bfloat16
AF = mybir.ActivationFunctionType
ALU = mybir.AluOpType
BF = ml_dtypes.bfloat16

S, C, H, D = 4096, 1024, 8, 128
NT, NST = 32, 8
CB = 12.0
SCL = 1.0 / np.sqrt(128.0)
LAGS = (0, 1, 2, 4, 8, 16)


def chunks_of(r):
    return [r - m for m in LAGS if r - m >= 0]


def mlog_np():
    m = masks_np()
    ml = np.where(m > 0, 0.0, -512.0).astype(np.float32)
    return np.ascontiguousarray(ml.transpose(2, 0, 1))


def masks_np():
    m = np.zeros((6, 128, 128), np.float32)
    for o in (0, 1, 2, 4, 8, 16, 32, 64):
        for i in range(128):
            p = i - o
            if p >= 0:
                m[0, p, i] = 1.0
            else:
                m[1, p + 128, i] = 1.0
    m[1] += np.eye(128, dtype=np.float32)
    for s in range(2, 6):
        m[s] = np.eye(128)
    return m


class Plan:
    def __init__(self, nc):
        self.nc = nc
        self.items = {"sync": [], "gpsimd": [], "tensor": [], "scalar": [], "vector": []}
        self.sem = {}
        self.val = {}

    def add_sem(self, name, stack):
        self.sem[name] = stack.enter_context(self.nc.semaphore(name))
        self.val[name] = 0

    def step(self, eng, fn, waits=(), inc=None, amt=1):
        if inc is not None:
            self.val[inc] += amt
        self.items[eng].append((fn, tuple(waits), inc, amt))
        return self.val[inc] if inc is not None else None

    def replay(self, eng_name, eng):
        for fn, waits, inc, amt in self.items[eng_name]:
            for s, v in waits:
                if v > 0:
                    eng.wait_ge(self.sem[s], v)
            last = fn(eng)
            if inc is not None:
                assert last is not None
                last.then_inc(self.sem[inc], amt)


def build_nc(mode="full", upto=99, nst_lim=99):
    nc = bass.Bass("TRN2", target_bir_lowering=False)

    x_d = nc.dram_tensor("x", [S, C], F32, kind="ExternalInput").ap()
    wqkv_d = nc.dram_tensor("wqkv", [C, 3 * C], F32, kind="ExternalInput").ap()
    wo_d = nc.dram_tensor("wo", [C, C], F32, kind="ExternalInput").ap()
    wrt_d = nc.dram_tensor("wrt", [C, C], F32, kind="ExternalInput").ap()
    wkt_d = nc.dram_tensor("wkt", [C, C], F32, kind="ExternalInput").ap()
    wvt_d = nc.dram_tensor("wvt", [C, C], F32, kind="ExternalInput").ap()
    cw_d = nc.dram_tensor("cw", [128, H, 7], F32, kind="ExternalInput").ap()
    mur_d = nc.dram_tensor("mur", [128, H], F32, kind="ExternalInput").ap()
    muk_d = nc.dram_tensor("muk", [128, H], F32, kind="ExternalInput").ap()
    y_d = nc.dram_tensor("y", [S, C], F32, kind="ExternalOutput").ap()

    q_sp = nc.dram_tensor("q_sp", [NST, 128, H, 512], BF16, kind="Internal").ap()
    k_sp = nc.dram_tensor("k_sp", [NST, 128, H, 512], BF16, kind="Internal").ap()
    vt_sp = nc.dram_tensor("vt_sp", [NST, 128, 4, H, 128], BF16, kind="Internal").ap()
    cv_sp = nc.dram_tensor("cv_sp", [NST, 128, H, 512], BF16, kind="Internal").ap()
    p_sp = nc.dram_tensor("p_sp", [NST, 128, 4, C], F32, kind="Internal").ap()

    id_t = nc.inline_tensor(np.eye(128, dtype=BF), "id_t")
    ones_t = nc.inline_tensor(np.ones((128, 128), dtype=BF), "ones_t")
    mlog_t = nc.inline_tensor(mlog_np().astype(BF), "mlog_t")
    cvals = np.zeros((128, 8), np.float32)
    cvals[:, 0], cvals[:, 1], cvals[:, 2], cvals[:, 3] = -CB, SCL, 1e-6, 1e-5
    cvals[:, 4] = -1.0
    con_t = nc.inline_tensor(cvals, "con_t")

    import contextlib
    stck = contextlib.ExitStack()
    SBASE = ((nc.sbuf_base + 31) // 32) * 32

    def mk_region(base):
        state = {"off": base}

        def alloc(name, shape, dtype):
            esz = 2 if dtype == BF16 else 4
            per = int(np.prod(shape[1:])) * esz
            off = state["off"]
            t = nc.alloc_sbuf_tensor_at(name, list(shape), dtype, offset=off)
            state["off"] = off + ((per + 31) // 32) * 32
            return t
        return alloc, state

    ap_, pstate = mk_region(SBASE)
    s_id = ap_("s_id", [128, 128], BF16)
    s_ones = ap_("s_ones", [128, 128], BF16)
    s_con = ap_("s_con", [128, 8], F32)
    s_cw = ap_("s_cw", [128, H, 7], F32)
    s_mur = ap_("s_mur", [128, H], F32)
    s_muk = ap_("s_muk", [128, H], F32)
    s_omr = ap_("s_omr", [128, H], F32)
    s_omk = ap_("s_omk", [128, H], F32)
    s_kvsb = ap_("s_kvsb", [128, H, 129], F32)
    carry_v = ap_("carry_v", [128, H, 6], BF16)
    carry_xn = ap_("carry_xn", [128, H, 1], BF16)
    s2_mlog = ap_("s2_mlog", [128, 6, 128], BF16)
    s2_fkr = ap_("s2_fkr", [128, H, 128], BF16)
    s2_kvb = ap_("s2_kvb", [128, H, 128], BF16)
    base_ov = pstate["off"]

    a1, st1 = mk_region(base_ov)
    s1_wq = a1("s1_wq", [128, 8, 3072], BF16)
    s1_wr = a1("s1_wr", [128, 8, C], BF16)
    s1_wk = a1("s1_wk", [128, 8, C], BF16)
    s1_wv = a1("s1_wv", [128, 8, C], BF16)
    s1_xa = a1("s1_xa", [128, 2, C], F32)
    s1_xb = a1("s1_xb", [128, C], F32)
    s1_xnb = a1("s1_xnb", [128, 4, C], BF16)
    s1_xnfm = a1("s1_xnfm", [128, 8, 513], BF16)
    qk_off = st1["off"]
    s1_qk = a1("s1_qk", [128, H, 512], BF16)
    s1_xb2 = nc.alloc_sbuf_tensor_at("s1_xb2", [128, C], F32, offset=qk_off)
    s1_fk = a1("s1_fk", [128, H, 512], BF16)
    s1_xr = a1("s1_xr", [128, H, 512], BF16)
    s1_rg = a1("s1_rg", [128, H, 512], BF16)
    s1_vh = a1("s1_vh", [128, H, 518], BF16)
    s1_vt = a1("s1_vt", [128, 4, H, 129], BF16)
    s1_fkt = a1("s1_fkt", [128, 4, H, 128], BF16)
    s1_cv = a1("s1_cv", [128, H, 512], BF16)
    s1_p = a1("s1_p", [128, 2, C], F32)
    s1_st = a1("s1_st", [128, 24], F32)
    end1 = st1["off"]

    a2, st2 = mk_region(base_ov)
    s2_kr = a2("s2_kr", [128, 18, H, 128], BF16)
    s2_vr = a2("s2_vr", [128, 18, H, 128], BF16)
    s2_wo = a2("s2_wo", [128, 8, C], BF16)
    s2_q = a2("s2_q", [128, 2, H, 512], BF16)
    s2_cv = a2("s2_cv", [128, 2, H, 512], BF16)
    s2_fq = a2("s2_fq", [128, H, 512], BF16)
    s2_fqs = a2("s2_fqs", [128, H, 512], BF16)
    s2_er = a2("s2_er", [128, 2, 768], BF16)
    s2_em = a2("s2_em", [128, 2, 768], BF16)
    s2_h = a2("s2_h", [128, H, 512], BF16)
    s2_rr = a2("s2_rr", [128, 1024], F32)
    s2_zb = a2("s2_zb", [128, 8, 512], F32)
    s2_scr = a2("s2_scr", [128, 512], BF16)
    s2_wos = a2("s2_wos", [128, 8, 512], BF16)
    s2_p = a2("s2_p", [128, 4, C], F32)
    end2 = st2["off"]
    sbuf_used = max(end1, end2)
    got = nc.bump_sbuf(sbuf_used - SBASE + 32)
    assert got is not None, f"SBUF over budget: used {sbuf_used}"

    PS = stck.enter_context(nc.psum_tensor("PS", [128, 4096], F32))
    PSb = PS[:].bitcast(BF16)

    def bank(b, c0, c1):
        return PS[:, 512 * b + c0: 512 * b + c1]

    pl = Plan(nc)
    for sname in ["d_w", "d_wc", "d_x0", "d_x1", "d_xb0", "d_xb1", "d_q", "d_k",
                  "d_vt", "d_cv", "d_p", "d_q2a", "d_q2b", "d_cv2a", "d_cv2b",
                  "d_p2a", "d_p2b", "d_chk", "d_chv", "d_y0", "d_y1", "d_y2", "d_y3",
                  "pe", "act", "dve", "sq", "pool"]:
        pl.add_sem(sname, stck)
    W = pl.step
    rtc = {"sq": 0}

    def cinc(inst):
        inst.then_inc(pl.sem["sq"], 1)
        rtc["sq"] += 1
        return inst

    def cwt(e):
        e.wait_ge(pl.sem["sq"], rtc["sq"])

    rtp = {"n": 0}

    def pinc(inst):
        inst.then_inc(pl.sem["pool"], 1)
        rtp["n"] += 1
        return inst

    def pwt(e):
        e.wait_ge(pl.sem["pool"], rtp["n"])

    W("sync", lambda e: e.dma_start(out=s_id[:], in_=id_t.ap()), inc="d_wc", amt=16)
    W("sync", lambda e: e.dma_start(out=s_ones[:], in_=ones_t.ap()), inc="d_wc", amt=16)
    W("sync", lambda e: e.dma_start(out=s_con[:], in_=con_t.ap()), inc="d_wc", amt=16)
    W("sync", lambda e: e.dma_start(out=s2_mlog[:], in_=mlog_t.ap()), inc="d_wc", amt=16)
    W("sync", lambda e: e.dma_start(out=s_cw[:], in_=cw_d), inc="d_wc", amt=16)
    W("sync", lambda e: e.dma_start(out=s_mur[:], in_=mur_d), inc="d_wc", amt=16)
    W("sync", lambda e: e.dma_start(out=s_muk[:], in_=muk_d), inc="d_wc", amt=16)
    W("gpsimd", lambda e: e.dma_start(out=s1_wq[:], in_=wqkv_d.rearrange("(k p) m -> p k m", p=128)), inc="d_w", amt=16)
    W("gpsimd", lambda e: e.dma_start(out=s1_wr[:], in_=wrt_d.rearrange("(k p) m -> p k m", p=128)),
      waits=[("d_w", 16)], inc="d_w", amt=16)
    W("gpsimd", lambda e: e.dma_start(out=s1_wk[:], in_=wkt_d.rearrange("(k p) m -> p k m", p=128)),
      waits=[("d_w", 32)], inc="d_w", amt=16)
    W("gpsimd", lambda e: e.dma_start(out=s1_wv[:], in_=wvt_d.rearrange("(k p) m -> p k m", p=128)),
      waits=[("d_w", 48)], inc="d_w", amt=16)
    wload = pl.val["d_w"]
    wcload = pl.val["d_wc"]

    def init_fn(e):
        e.tensor_scalar(s_omr[:], s_mur[:], -1.0, 1.0, ALU.mult, ALU.add)
        e.tensor_scalar(s_omk[:], s_muk[:], -1.0, 1.0, ALU.mult, ALU.add)
        e.memset(s_kvsb[:], 0.0)
        e.memset(carry_v[:], 0.0)
        e.memset(carry_xn[:], 0.0)
        e.memset(s1_vh[:, :, 0:6], 0.0)
        e.memset(s1_xnfm[:, :, 0:1], 0.0)
        return e.memset(s1_vt[:], 1.0)
    W("vector", init_fn, waits=[("d_w", wload), ("d_wc", wcload)], inc="dve")
    init_done = pl.val["dve"]

    ln_done = [0] * NT
    lnf_done = [0] * NT
    xnT_evac = [0] * NT
    xnT_pe = [0] * NT
    qkv_evac = {}
    qkv_pe = [0] * NST
    rmm_pe = [0] * NST
    kmm_pe = [0] * NST
    vmm_pe = [0] * NST
    trans_pe = [0] * NST
    vT_pe = [0] * NST
    kv_pe = [0] * NST
    convh_done = [[0] * H for _ in range(NST)]
    carry_ci = [0] * NST
    fk_scr_done = [0] * NST
    kve_done = [0] * NST
    vt_done = [0] * NST
    ffn_tt_last = [(0, 0, 0, 0)] * NST
    BNK4 = (0, 1, 2, 7)
    pas_last = [0] * NST
    p_spilled = [0] * NST
    carry_in = [0] * NST
    xk_view = s1_fkt[:].rearrange("p r h t -> p (r h t)").rearrange("p (h t) -> p h t", h=H)
    pre = {}

    def emit_carry_in(st):
        def ci_fn(e):
            for ch in range(H - 1):
                e.tensor_copy(s1_vh[:, ch, 0:6], carry_v[:, ch, :])
            return e.tensor_copy(s1_vh[:, H - 1, 0:6], carry_v[:, H - 1, :])
        W("vector", ci_fn,
          waits=[("act", qkv_evac[(st, 23)]),
                 ("dve", convh_done[st - 1][7] if st > 0 else 0)], inc="dve")
        carry_ci[st] = pl.val["dve"]

    def emit_conv_head(st, ch, extra=()):
        def conv_fn(e, st=st, ch=ch):
            for j in range(7):
                src = s1_vh[:, ch, j:512 + j]
                if j == 0:
                    cinc(e.tensor_scalar(s1_cv[:, ch, :], src, s_cw[:, ch, j:j + 1], None, ALU.mult))
                else:
                    cinc(e.scalar_tensor_tensor(s1_cv[:, ch, :], src, s_cw[:, ch, j:j + 1],
                                                s1_cv[:, ch, :], ALU.mult, ALU.add))
                cwt(e)
            return e.tensor_copy(carry_v[:, ch, :], s1_vh[:, ch, 512:518])
        wts = [("act", qkv_evac[(st, 16 + ch)]), ("d_cv", 16 * st),
               ("dve", carry_ci[st])] + list(extra)
        W("vector", conv_fn, waits=wts, inc="dve")
        convh_done[st][ch] = pl.val["dve"]

    def emit_conv_head_act(st, ch, ft_pe):
        ujd = {}
        addd = {}

        def t0_fn(e, st=st, ch=ch):
            return e.tensor_scalar(s1_cv[:, ch, :], s1_vh[:, ch, 0:512],
                                   s_cw[:, ch, 0:1], None, ALU.mult)
        W("vector", t0_fn,
          waits=[("act", qkv_evac[(st, 16 + ch)]), ("d_cv", 16 * st),
                 ("dve", carry_ci[st])], inc="dve")
        addd[0] = pl.val["dve"]
        for j in range(1, 7):
            wts = [("dve", addd[j - 2] if j >= 2 else 0), ("pe", ft_pe)]
            if j <= 2 and ch > 4:
                wts.append(("dve", convh_done[st][ch - 1]))
            W("scalar", (lambda st, ch, j: lambda e: e.activation(
                s1_rg[:, j % 2, :], s1_vh[:, ch, j:512 + j], AF.Identity,
                scale=s_cw[:, ch, j:j + 1]))(st, ch, j), waits=wts, inc="act")
            ujd[j] = pl.val["act"]

            def add_fn(e, st=st, ch=ch, j=j):
                if j < 6:
                    return e.tensor_tensor(s1_cv[:, ch, :], s1_cv[:, ch, :],
                                           s1_rg[:, j % 2, :], ALU.add)
                cinc(e.tensor_tensor(s1_cv[:, ch, :], s1_cv[:, ch, :],
                                     s1_rg[:, j % 2, :], ALU.add))
                cwt(e)
                return e.tensor_copy(carry_v[:, ch, :], s1_vh[:, ch, 512:518])
            W("vector", add_fn, waits=[("act", ujd[j]), ("dve", addd[j - 1])], inc="dve")
            addd[j] = pl.val["dve"]
        convh_done[st][ch] = pl.val["dve"]

    def emit_ln_a(st):
        pp = st % 2
        vars_ = [0] * 4
        for rt in range(4):
            r = 4 * st + rt
            sl = rt % 2
            xs = "d_x0" if rt % 2 == 0 else "d_x1"
            W("sync", (lambda sl, r: lambda e: e.dma_start(
                out=s1_xa[:, sl, :], in_=x_d[128 * r:128 * (r + 1), :]))(sl, r),
                waits=[("dve", lnf_done[r - 2] if r >= 2 else 0)], inc=xs, amt=16)
            xw = (xs, pl.val[xs])

            def ln_fn(e, sl=sl, rt=rt, pp=pp):
                xin = s1_xa[:, sl, :]
                cinc(e.scalar_tensor_tensor(s1_xnb[:, rt, :], xin, 1.0, xin, ALU.mult, ALU.mult, accum_out=s1_st[:, 1:2]))
                cwt(e)
                cinc(e.tensor_scalar(s1_xnb[:, rt, :], xin, 1.0, None, ALU.mult, ALU.add, accum_out=s1_st[:, 0:1]))
                cwt(e)
                mu = s1_st[:, 8 + 4 * pp + rt:9 + 4 * pp + rt]
                cinc(e.tensor_scalar(mu, s1_st[:, 0:1], 1.0 / C, None, ALU.mult))
                e.tensor_scalar(s1_st[:, 2:3], s1_st[:, 1:2], 1.0 / C, None, ALU.mult)
                cwt(e)
                cinc(e.tensor_scalar(s1_st[:, 3:4], mu, mu, None, ALU.mult))
                cwt(e)
                return e.tensor_tensor(s1_st[:, 4 + rt:5 + rt], s1_st[:, 2:3], s1_st[:, 3:4], ALU.subtract)
            W("vector", ln_fn, waits=[xw, ("dve", init_done),
                                      ("pe", xnT_pe[r - 4] if r >= 4 else 0)], inc="dve")
            vars_[rt] = pl.val["dve"]
            lnf_done[r] = pl.val["dve"]
        return vars_

    def emit_ln_sqrt(st, vars_):
        stds = [0] * 4
        for rt in range(4):
            W("scalar", (lambda rt: lambda e: e.activation(
                s1_st[:, 4 + rt:5 + rt],
                s1_st[:, 4 + rt:5 + rt], AF.Sqrt, bias=s_con[:, 3:4]))(rt),
                waits=[("dve", vars_[rt])], inc="act")
            stds[rt] = pl.val["act"]
        return stds

    def emit_ln_norm(st, stds):
        pp = st % 2
        for rt in range(4):
            r = 4 * st + rt
            sl = rt % 2

            def ln2_fn(e, sl=sl, rt=rt, pp=pp):
                mu = s1_st[:, 8 + 4 * pp + rt:9 + 4 * pp + rt]
                rs = s1_st[:, 16 + 4 * pp + rt:17 + 4 * pp + rt]
                cinc(e.reciprocal(rs, s1_st[:, 4 + rt:5 + rt]))
                cwt(e)
                cinc(e.tensor_scalar(s1_xnb[:, rt, :], s1_xnb[:, rt, :], mu, rs,
                                     ALU.subtract, ALU.mult))
                cinc(e.tensor_tensor(mu, mu, rs, ALU.mult))
                cwt(e)
                return e.tensor_scalar(rs, rs, 1.0, None, ALU.add)
            W("vector", ln2_fn, waits=[("act", stds[rt])], inc="dve")
            ln_done[r] = pl.val["dve"]

    def emit_xnt(st, after_qkv):
        for rt in range(4):
            r = 4 * st + rt
            for half in range(2):
                xbk = 5 if half == 0 else 6

                def xnT_fn(e, rt=rt, xbk=xbk, half=half):
                    last = None
                    for cc in range(4):
                        c = 4 * half + cc
                        last = e.transpose(
                            PSb[:, 1024 * xbk + 128 * cc: 1024 * xbk + 128 * cc + 128],
                            s1_xnb[:, rt, 128 * c:128 * (c + 1)], s_id[:])
                    return last
                prev = xnT_evac[r - 1] if r >= 1 else 0
                W("tensor", xnT_fn, waits=[("dve", ln_done[r]), ("act", prev),
                                           ("dve", pas_last[st - 2] if st >= 2 else 0)], inc="pe")
                tdone = pl.val["pe"]

                def xnTe_fn(e, rt=rt, xbk=xbk, half=half):
                    return e.copy(
                        s1_xnfm[:, 4 * half:4 * half + 4, 1 + 128 * rt: 1 + 128 * rt + 128],
                        PSb[:, 1024 * xbk: 1024 * xbk + 512].rearrange("p (c t) -> p c t", c=4))
                wts = [("pe", tdone)]
                if rt == 0 and half == 0 and st > 0:
                    wts.append(("pe", after_qkv))
                W("scalar", xnTe_fn, waits=wts, inc="act")
            xnT_pe[r] = pl.val["pe"]
            xnT_evac[r] = pl.val["act"]
        W("vector", lambda e: e.tensor_copy(s1_xnfm[:, :, 0:1], carry_xn[:]),
          waits=[("act", xnT_evac[4 * st + 3])], inc="dve")
        carry_in[st] = pl.val["dve"]

    vars0 = emit_ln_a(0)
    stds0 = emit_ln_sqrt(0, vars0)
    emit_ln_norm(0, stds0)
    emit_xnt(0, 0)
    QORD = list(range(16, 24)) + list(range(0, 8)) + list(range(8, 16))
    for st in range(NST):
        if st >= nst_lim:
            continue
        carry_in_done = carry_in[st]

        if st + 1 < NST and st + 1 < nst_lim:
            nvars = emit_ln_a(st + 1)

        def d_fn(e):
            return e.tensor_tensor(s1_xr[:], s1_xnfm[:, :, 1:513],
                                   s1_xnfm[:, :, 0:512], ALU.subtract)
        W("vector", d_fn,
          waits=[("dve", carry_in_done), ("pe", vmm_pe[st - 1] if st > 0 else 0)], inc="dve")
        d_done = pl.val["dve"]

        def fxk_fn(e):
            last = None
            for ch in range(H):
                last = e.scalar_tensor_tensor(xk_view[:, ch, :], s1_xr[:, ch, :], s_muk[:, ch:ch + 1],
                                              s1_xnfm[:, ch, 0:512], ALU.mult, ALU.add)
            return last
        W("vector", fxk_fn,
          waits=[("dve", d_done), ("pe", kv_pe[st - 1] if st > 0 else 0)], inc="dve")
        xk_done = pl.val["dve"]

        def fxr_fn(e):
            last = None
            for ch in range(H):
                last = e.scalar_tensor_tensor(s1_xr[:, ch, :], s1_xr[:, ch, :], s_mur[:, ch:ch + 1],
                                              s1_xnfm[:, ch, 0:512], ALU.mult, ALU.add)
            return last
        W("vector", fxr_fn, waits=[("dve", xk_done)], inc="dve")
        xr_done = pl.val["dve"]

        W("vector", lambda e: e.tensor_copy(carry_xn[:], s1_xnfm[:, :, 512:513]),
          waits=[("dve", xr_done)], inc="dve")

        fkmin = {}
        fkexp = {}
        fkadd = {}
        for i, m in enumerate(QORD):
            bsel = BNK4[i % 4]

            def qkv_fn(e, m=m, bsel=bsel):
                last = None
                for kc in range(8):
                    last = e.matmul(bank(bsel, 0, 512), s1_wq[:, kc, 128 * m:128 * (m + 1)],
                                    s1_xnfm[:, kc, 1:513], start=(kc == 0), stop=(kc == 7),
                                    skip_group_check=True)
                return last
            waits = [("dve", carry_in_done), ("d_w", wload)]
            if i >= 4:
                waits.append(("act", qkv_evac[(st, QORD[i - 4])]))
            elif st > 0:
                waits.append(("dve", ffn_tt_last[st - 1][i]))
            W("tensor", qkv_fn, waits=waits, inc="pe")
            mm_done = pl.val["pe"]

            if m >= 16:
                dst = s1_vh[:, m - 16, 6:518]
                extra = [("dve", convh_done[st - 1][m - 16] if st > 0 else 0)]
            elif m < 8:
                dst = s1_qk[:, m, :]
                extra = [("d_q", 16 * st), ("d_k", 16 * st),
                         ("dve", fk_scr_done[st - 1] if st > 0 else 0),
                         ("dve", pas_last[st - 1] if st > 0 else 0)]
            else:
                dst = s1_qk[:, m - 8, :]
                extra = [("d_q", 16 * (st + 1))]
            W("scalar", (lambda dst, bsel: lambda e: e.copy(dst, bank(bsel, 0, 512)))(dst, bsel),
              waits=[("pe", mm_done)] + extra, inc="act")
            qkv_evac[(st, m)] = pl.val["act"]

            if m == 4 and st + 1 < NST and st + 1 < nst_lim:
                nstds = emit_ln_sqrt(st + 1, nvars)
            if m == 7:
                W("gpsimd", (lambda st: lambda e: e.dma_start(out=q_sp[st], in_=s1_qk[:]))(st),
                  waits=[("act", qkv_evac[(st, 7)]), ("d_q", 16 * st)], inc="d_q", amt=16)
            if 8 <= m < 16:
                ch = m - 8
                wts = [("act", qkv_evac[(st, m)])]
                if ch == 0 and st > 0:
                    wts.append(("pe", trans_pe[st - 1]))
                W("scalar", (lambda ch: lambda e: e.activation(
                    s1_fk[:, ch, :], s1_qk[:, ch, :], AF.Exp))(ch),
                    waits=wts, inc="act")
                fkexp[ch] = pl.val["act"]

                def fka_fn(e, ch=ch):
                    cinc(e.tensor_scalar(s1_rg[:, 6 + ch % 2, :], s1_qk[:, ch, :], 0.0, None, ALU.max))
                    cwt(e)
                    return e.scalar_tensor_tensor(s1_fk[:, ch, :], s1_fk[:, ch, :], 1.0,
                                                  s1_rg[:, 6 + ch % 2, :], ALU.min, ALU.add)
                wts = [("act", fkexp[ch])]
                if ch >= 2:
                    wts.append(("dve", fkadd[ch - 2]))
                W("vector", fka_fn, waits=wts, inc="dve")
                fkadd[ch] = pl.val["dve"]
                if m == 9 and st + 1 < NST and st + 1 < nst_lim:
                    emit_ln_norm(st + 1, nstds)
            if m == 15:
                W("gpsimd", (lambda st: lambda e: e.dma_start(out=k_sp[st], in_=s1_qk[:]))(st),
                  waits=[("act", qkv_evac[(st, 15)]), ("d_k", 16 * st)], inc="d_k", amt=16)
        qkv_pe[st] = pl.val["pe"]
        fk_scr_done[st] = fkadd[7]

        emit_carry_in(st)

        if st + 1 < NST and st + 1 < nst_lim:
            emit_xnt(st + 1, qkv_pe[st])

        emit_conv_head(st, 0)

        rg_evac = {}
        for m in range(8):
            bsel = BNK4[m % 4]

            def rmm_fn(e, m=m, bsel=bsel):
                last = None
                for kc in range(8):
                    last = e.matmul(bank(bsel, 0, 512), s1_wr[:, kc, 128 * m:128 * (m + 1)],
                                    s1_xr[:, kc, :], start=(kc == 0), stop=(kc == 7),
                                    skip_group_check=True)
                return last
            wts = [("dve", xr_done)]
            wts.append(("act", rg_evac[m - 4]) if m >= 4 else ("act", qkv_evac[(st, QORD[20 + m])]))
            W("tensor", rmm_fn, waits=wts, inc="pe")
            rmm = pl.val["pe"]
            sg_w = [("pe", rmm)]
            if m >= 4:
                sg_w.append(("dve", fk_scr_done[st]))
            W("scalar", (lambda m, bsel: lambda e: e.activation(
                s1_rg[:, m, :], bank(bsel, 0, 512), AF.Sigmoid))(m, bsel),
                waits=sg_w, inc="act")
            rg_evac[m] = pl.val["act"]
        rmm_pe[st] = pl.val["pe"]

        emit_conv_head(st, 1)

        ka_evac = {}
        mx_done = {}
        sq_done = {}
        for m in range(8):
            bsel = BNK4[m % 4]

            def kmm_fn(e, m=m, bsel=bsel):
                last = None
                for kc in range(8):
                    last = e.matmul(bank(bsel, 0, 512), s1_wk[:, kc, 128 * m:128 * (m + 1)],
                                    xk_view[:, kc, :], start=(kc == 0), stop=(kc == 7),
                                    skip_group_check=True)
                return last
            wts = [("dve", xk_done)]
            wts.append(("act", ka_evac[m - 4]) if m >= 4 else ("act", rg_evac[4 + m]))
            W("tensor", kmm_fn, waits=wts, inc="pe")
            kmm = pl.val["pe"]
            W("scalar", (lambda m, bsel: lambda e: e.activation(
                s1_xr[:, m, :], bank(bsel, 0, 512), AF.Relu))(m, bsel),
                waits=[("pe", kmm)], inc="act")
            ka_evac[m] = pl.val["act"]
            W("gpsimd", (lambda m: lambda e: e.tensor_tensor(
                s1_xr[:, m, :], s1_xr[:, m, :], s1_xr[:, m, :], ALU.mult))(m),
                waits=[("act", ka_evac[m])], inc="pool")
            sq_done[m] = pl.val["pool"]
            if m == 3:
                emit_conv_head(st, 2)
            if m == 7:
                emit_conv_head(st, 3)
        kmm_pe[st] = pl.val["pe"]

        ffn_tt = {}
        for m in range(8):
            bsel = BNK4[m % 4]

            def vmm_fn(e, m=m, bsel=bsel):
                last = None
                for kc in range(8):
                    last = e.matmul(bank(bsel, 0, 512), s1_wv[:, kc, 128 * m:128 * (m + 1)],
                                    s1_xr[:, kc, :], start=(kc == 0), stop=(kc == 7),
                                    skip_group_check=True)
                return last
            wts = [("pool", sq_done[7])]
            if m >= 4:
                wts.append(("dve", ffn_tt[m - 4]))
            else:
                wts.append(("act", ka_evac[4 + m]))
            W("tensor", vmm_fn, waits=wts, inc="pe")
            vmm = pl.val["pe"]
            W("vector", (lambda m, bsel: lambda e: e.tensor_tensor(
                s1_rg[:, m, :], s1_rg[:, m, :], bank(bsel, 0, 512), ALU.mult))(m, bsel),
                waits=[("pe", vmm)], inc="dve")
            ffn_tt[m] = pl.val["dve"]
        vmm_pe[st] = pl.val["pe"]
        ffn_tt_last[st] = (ffn_tt[4], ffn_tt[5], ffn_tt[6], ffn_tt[7])

        tr_evac = 0
        for which, src, dstbuf in (("v", s1_vh, s1_vt), ("f", s1_fk, s1_fkt)):
            for rt in range(4):
                slot = rt % 2

                def trT_fn(e, src=src, rt=rt, slot=slot, which=which):
                    last = None
                    for hh in range(8):
                        if which == "v":
                            inp = src[:, hh, 6 + 128 * rt: 6 + 128 * rt + 128]
                        else:
                            inp = src[:, hh, 128 * rt: 128 * rt + 128]
                        last = e.transpose(
                            PSb[:, 1024 * (5 + slot) + 128 * hh: 1024 * (5 + slot) + 128 * hh + 128],
                            inp, s_id[:])
                    return last
                wts = [("dve", fk_scr_done[st] if which == "f" else 0), ("act", tr_evac)]
                if which == "v" and rt == 0:
                    wts.append(("act", qkv_evac[(st, 23)]))
                    wts.append(("dve", pas_last[st - 1] if st > 0 else 0))
                    if st + 1 < NST:
                        wts.append(("act", xnT_evac[4 * (st + 1) + 3]))
                    wts.append(("dve", carry_ci[st]))
                W("tensor", trT_fn, waits=wts, inc="pe")
                t_done = pl.val["pe"]

                def trTe_fn(e, dstbuf=dstbuf, rt=rt, slot=slot, which=which):
                    dst = dstbuf[:, rt, :, 0:128] if which == "v" else dstbuf[:, rt, :, :]
                    return e.copy(
                        dst, PSb[:, 1024 * (5 + slot): 1024 * (5 + slot) + 1024].rearrange("p (h t) -> p h t", h=8))
                wts = [("pe", t_done)]
                if which == "v" and rt == 0:
                    wts.append(("d_vt", 16 * st))
                if which == "f" and rt == 0:
                    wts.append(("pe", kmm_pe[st]))
                W("scalar", trTe_fn, waits=wts, inc="act")
                tr_evac = pl.val["act"]
            if which == "v":
                vT_pe[st] = pl.val["pe"]
        trans_pe[st] = pl.val["pe"]
        vt_done[st] = tr_evac
        W("gpsimd", (lambda st: lambda e: e.dma_start(out=vt_sp[st], in_=s1_vt[:, :, :, 0:128]))(st),
          waits=[("act", vt_done[st]), ("d_vt", 16 * st)], inc="d_vt", amt=16)

        kvadd = {}
        for hh in range(H):
            bsel = 3 + (hh % 2)

            def kv_fn(e, hh=hh, bsel=bsel):
                last = None
                for rt in range(4):
                    last = e.matmul(bank(bsel, 0, 129), s1_fkt[:, rt, hh, :],
                                    s1_vt[:, rt, hh, 0:129], start=(rt == 0), stop=(rt == 3),
                                    skip_group_check=True)
                return last
            wts = [("act", vt_done[st])]
            if hh >= 2:
                wts.append(("dve", kvadd[hh - 2]))
            elif st > 0:
                wts.append(("dve", kve_done[st - 1]))
            else:
                wts.append(("dve", init_done))
            W("tensor", kv_fn, waits=wts, inc="pe")
            kvmm = pl.val["pe"]
            W("vector", (lambda hh, bsel: lambda e: e.tensor_tensor(
                s_kvsb[:, hh, :], s_kvsb[:, hh, :], bank(bsel, 0, 129), ALU.add))(hh, bsel),
                waits=[("pe", kvmm)], inc="dve")
            kvadd[hh] = pl.val["dve"]
        kve_done[st] = kvadd[7]
        kv_pe[st] = pl.val["pe"]

        if st == NST - 1 and mode == "full" and upto >= 99 and nst_lim >= 99:
            W("sync", lambda e: e.dma_start(out=s2_kr[:, 0, :, :], in_=k_sp[0, :, :, 0:128]),
              waits=[("d_k", 16), ("pe", qkv_pe[st])], inc="d_chk", amt=16)
            pre["ch0k"] = pl.val["d_chk"]
            W("sync", lambda e: e.dma_start(out=s2_vr[:, 0, :, :], in_=vt_sp[0, :, 0, :, :]),
              waits=[("d_vt", 16), ("pe", qkv_pe[st])], inc="d_chv", amt=16)
            pre["ch0v"] = pl.val["d_chv"]
            W("sync", lambda e: e.dma_start(out=s2_q[:, 0, :, :], in_=q_sp[0]),
              waits=[("d_q", 16), ("dve", lnf_done[NT - 1]), ("pe", vmm_pe[st])],
              inc="d_q2a", amt=16)
            pre["q2"] = ("d_q2a", pl.val["d_q2a"])

        pas_arr = [0] * 4
        pp = st % 2
        prev_pas = pas_last[st - 1] if st > 0 else 0
        for rt in range(4):
            g = 4 * st + rt
            slot = rt % 2
            xbuf = s1_xb if rt % 2 == 0 else s1_xb2

            def fT_fn(e, rt=rt, slot=slot):
                last = None
                for c in range(8):
                    last = e.transpose(
                        PSb[:, 1024 * (5 + slot) + 128 * c: 1024 * (5 + slot) + 128 * c + 128],
                        s1_rg[:, c, 128 * rt:128 * rt + 128], s_id[:])
                return last
            wts = [("dve", ffn_tt[7]), ("pe", kv_pe[st]),
                   ("dve", pas_arr[rt - 2] if rt >= 2 else prev_pas)]
            W("tensor", fT_fn, waits=wts, inc="pe")
            ftd = pl.val["pe"]

            lwts = [("d_xb0" if rt % 2 == 0 else "d_xb1", 16 * (g // 2))]
            lwts.append(("dve", pas_arr[rt - 2] if rt >= 2 else prev_pas))
            if rt == 1:
                lwts.append(("d_k", 16 * (st + 1)))
                lwts.append(("dve", fk_scr_done[st]))
            W("sync", (lambda rt, xbuf=xbuf, r=4 * st + rt: lambda e: e.dma_start(
                out=xbuf[:], in_=x_d[128 * r:128 * (r + 1), :]))(rt),
                waits=lwts, inc=("d_xb0" if rt % 2 == 0 else "d_xb1"), amt=16)
            xbl = (("d_xb0" if rt % 2 == 0 else "d_xb1"),
                   pl.val["d_xb0" if rt % 2 == 0 else "d_xb1"])

            def pas_fn(e, rt=rt, slot=slot, pp=pp, xbuf=xbuf):
                cinc(e.tensor_scalar(s1_p[:, rt % 2, :], xbuf[:],
                                     s1_st[:, 16 + 4 * pp + rt:17 + 4 * pp + rt],
                                     s1_st[:, 8 + 4 * pp + rt:9 + 4 * pp + rt],
                                     ALU.mult, ALU.subtract))
                cwt(e)
                return e.tensor_tensor(s1_p[:, rt % 2, :], s1_p[:, rt % 2, :],
                                       PSb[:, 1024 * (5 + slot): 1024 * (5 + slot) + 1024], ALU.add)
            W("vector", pas_fn,
              waits=[("pe", ftd), xbl,
                     ("d_p", 16 * (g - 1) if g >= 1 else 0)], inc="dve")
            pas_arr[rt] = pl.val["dve"]

            W("gpsimd", (lambda st, rt, g=g: lambda e: e.dma_start(
                out=p_sp[st, :, rt, :], in_=s1_p[:, rt % 2, :]))(st, rt),
                waits=[("dve", pas_arr[rt]), ("d_p", 16 * g)], inc="d_p", amt=16)
            if rt == 1:
                emit_conv_head(st, 4)
            if rt == 2:
                emit_conv_head(st, 5)
            if rt == 3:
                emit_conv_head(st, 6)
                emit_conv_head(st, 7)
        pas_last[st] = pas_arr[3]
        p_spilled[st] = pl.val["d_p"]
        W("gpsimd", (lambda st: lambda e: e.dma_start(out=cv_sp[st], in_=s1_cv[:]))(st),
          waits=[("dve", convh_done[st][7]), ("d_cv", 16 * st)], inc="d_cv", amt=16)


    sweep1_p = p_spilled[NST - 1]

    if upto < 99 or nst_lim < 99:
        for engn in pl.items:
            pl.items[engn] = [
                (fn, tuple((sn, min(v, pl.val[sn])) for sn, v in waits), inc, amt)
                for fn, waits, inc, amt in pl.items[engn]]
        drains = [(sn, pl.val[sn]) for sn in pl.val if pl.val[sn] > 0 and sn != "sq"]
        W("gpsimd", lambda e: e.dma_start(out=y_d[0:128, :], in_=s1_xa[:, 0, :]),
          waits=drains, inc="d_y0", amt=16)
        W("sync", lambda e: None, waits=[("d_y0", 16)])
        with nc.Block() as blk:
            blk.sync(lambda e: pl.replay("sync", e))
            blk.gpsimd(lambda e: pl.replay("gpsimd", e))
            blk.tensor(lambda e: pl.replay("tensor", e))
            blk.scalar(lambda e: pl.replay("scalar", e))
            blk.vector(lambda e: pl.replay("vector", e))
        stck.close()
        return nc, sbuf_used

    if mode == "sweep1":
        for st in range(NST):
            W("sync", (lambda st: lambda e: e.dma_start(
                out=y_d[512 * st:512 * (st + 1), :].rearrange("(r p) c -> p r c", p=128),
                in_=p_sp[st]))(st),
                waits=[("d_p", p_spilled[st]), ("d_y0", 16 * st)], inc="d_y0", amt=16)
        W("sync", lambda e: None, waits=[("d_y0", 16 * NST)])
        with nc.Block() as blk:
            blk.sync(lambda e: pl.replay("sync", e))
            blk.gpsimd(lambda e: pl.replay("gpsimd", e))
            blk.tensor(lambda e: pl.replay("tensor", e))
            blk.scalar(lambda e: pl.replay("scalar", e))
            blk.vector(lambda e: pl.replay("vector", e))
        stck.close()
        return nc, sbuf_used

    W("gpsimd", lambda e: e.dma_start(out=s2_wo[:], in_=wo_d.rearrange("(k p) m -> p k m", p=128)),
      waits=[("d_w", 64), ("pe", vmm_pe[NST - 1]), ("d_vt", 16 * NST)], inc="d_w", amt=16)
    wo_loaded = pl.val["d_w"]

    def mid_fn(e):
        last = None
        for hh in range(H):
            e.tensor_copy(s2_kvb[:, hh, :], s_kvsb[:, hh, 0:128])
            last = e.tensor_scalar(s2_fkr[:, hh, :], s_ones[:], s_kvsb[:, hh, 128:129], None, ALU.mult)
        return last
    W("vector", mid_fn, waits=[("dve", kve_done[NST - 1])], inc="dve")
    mid_done = pl.val["dve"]

    sc_done, exp_done, msk_done, tele_done = {}, {}, {}, {}
    norm_done, norm_half, ch_load = {}, {}, {}
    q2l, cv2l = [0] * NST, [0] * NST
    p2l = [0] * NT
    h_done = [0] * NST
    wo_done = [0] * NST
    out_done = [0] * NT

    def att_i(r, hh):
        return r * 8 + hh

    def chload_fns(r):
        st_, rm = r // 4, r % 4

        def fk_(e):
            return e.dma_start(out=s2_kr[:, r % 18, :, :], in_=k_sp[st_, :, :, 128 * rm:128 * rm + 128])

        def fv_(e):
            return e.dma_start(out=s2_vr[:, r % 18, :, :], in_=vt_sp[st_, :, rm, :, :])
        return fk_, fv_

    ch_load[0] = (("d_chk", pre["ch0k"]), ("d_chv", pre["ch0v"]))

    for st in range(NST):
        sl = st % 2
        q2s = "d_q2a" if sl == 0 else "d_q2b"
        cv2s = "d_cv2a" if sl == 0 else "d_cv2b"
        if st == 0:
            q2l[0] = pre["q2"]
        else:
            q2w = [("d_q", 16 * (st + 1))]
            if st >= 2:
                q2w.append(("dve", h_done[st - 2]))
            else:
                q2w.append(("dve", lnf_done[NT - 1]))
                q2w.append(("pe", vmm_pe[NST - 1]))
            W("sync", (lambda st, sl: lambda e: e.dma_start(out=s2_q[:, sl, :, :], in_=q_sp[st]))(st, sl),
              waits=q2w, inc=q2s, amt=16)
            q2l[st] = (q2s, pl.val[q2s])
        cv2w = [("d_cv", 16 * (st + 1))]
        if st >= 2:
            cv2w.append(("dve", h_done[st - 2]))
        else:
            cv2w.append(("dve", pas_last[NST - 1]))
            cv2w.append(("pe", qkv_pe[NST - 1]))
        cv2w.append((cv2s, pl.val[cv2s]))
        W("gpsimd",
          (lambda st, sl: lambda e: e.dma_start(out=s2_cv[:, sl, :, :], in_=cv_sp[st]))(st, sl),
          waits=cv2w, inc=cv2s, amt=16)
        cv2l[st] = (cv2s, pl.val[cv2s])

        for rt in range(4):
            r = 4 * st + rt
            p2s = "d_p2a" if rt % 2 == 0 else "d_p2b"
            wts = [("d_p", sweep1_p)]
            if st == 0 and rt < 2:
                wts.append(("pe", kv_pe[NST - 1]))
                wts.append(("d_vt", 16 * NST))
                wts.append(("d_cv", 16 * NST))
            if r >= 4:
                wts.append((("d_y0", "d_y1", "d_y2", "d_y3")[rt], 16 * st))
            wts.append((p2s, pl.val[p2s]))
            W("gpsimd", (lambda st, rt: lambda e: e.dma_start(
                out=s2_p[:, rt, :], in_=p_sp[st, :, rt, :]))(st, rt),
                waits=wts, inc=p2s, amt=16)
            p2l[r] = (p2s, pl.val[p2s])

        fqf = s2_fq[:].rearrange("p h t -> p (h t)")
        fqsf = s2_fqs[:].rearrange("p h t -> p (h t)")
        qf = s2_q[:, sl, :, :].rearrange("p h t -> p (h t)")
        f1w = [q2l[st]]
        if st >= 1:
            f1w.append(("dve", h_done[st - 1]))
        else:
            f1w.append(("dve", pas_last[NST - 1]))
            f1w.append(("pe", trans_pe[NST - 1]))
        W("vector", (lambda qf: lambda e: e.tensor_scalar(fqsf, qf, 0.0, None, ALU.min))(qf),
          waits=f1w, inc="dve")
        f1 = pl.val["dve"]
        W("scalar", lambda e: e.activation(fqf, fqsf, AF.Exp), waits=[("dve", f1)], inc="act")
        f2 = pl.val["act"]

        def fq3_fn(e, qf=qf):
            cinc(e.tensor_scalar(fqsf, qf, 0.0, None, ALU.max))
            cwt(e)
            return e.tensor_tensor(fqf, fqf, fqsf, ALU.add)
        W("vector", fq3_fn, waits=[("act", f2)], inc="dve")
        fq_done = pl.val["dve"]

        pend = [None]

        def emit_te(p):
            r2, rt2, hh2, cls2, n2, eslot2, i2 = p

            def te_fn(e, hh=hh2, cls=cls2, n=n2, eslot=eslot2):
                tb = 4 + hh // 4
                rb = 6 + hh // 4
                tcol = 128 * (hh % 4)
                last = None
                for ci, c in enumerate(cls):
                    last = e.matmul(bank(tb, tcol, tcol + 128), s2_vr[:, c % 18, hh, :],
                                    s2_er[:, eslot, 128 * ci:128 * (ci + 1)],
                                    start=(ci == 0), stop=(ci == n - 1), skip_group_check=True)
                for ci in range(n):
                    last = e.matmul(bank(rb, tcol, tcol + 128), s_ones[:],
                                    s2_er[:, eslot, 128 * ci:128 * (ci + 1)],
                                    start=(ci == 0), stop=(ci == n - 1), skip_group_check=True)
                return last
            wts = [("act", exp_done[i2]), ch_load[r2][0], ch_load[r2][1]]
            if r2 >= 1:
                wts.append(("dve", norm_half.get((r2 - 1, hh2 // 4), 0)))
            if rt2 == 0 and hh2 == 0 and st > 0:
                wts.append(("dve", out_done[4 * st - 1]))
            W("tensor", te_fn, waits=wts, inc="pe")
            tele_done[i2] = pl.val["pe"]

            if hh2 == 3 or hh2 == 7:
                half = hh2 // 4

                def nr_fn(e, rt=rt2, half=half):
                    cinc(e.reciprocal(s2_rr[:, 512 * half:512 * half + 512],
                                      PS[:, 512 * (6 + half): 512 * (6 + half) + 512]))
                    cwt(e)
                    last = None
                    for hh3 in range(4 * half, 4 * half + 4):
                        last = e.tensor_tensor(s2_h[:, hh3, 128 * rt:128 * rt + 128],
                                               bank(4 + half, 128 * (hh3 % 4), 128 * (hh3 % 4) + 128),
                                               s2_rr[:, 128 * hh3:128 * hh3 + 128], ALU.mult)
                    return last
                W("vector", nr_fn, waits=[("pe", tele_done[i2])], inc="dve")
                norm_half[(r2, half)] = pl.val["dve"]
            if hh2 == 7:
                norm_done[r2] = norm_half[(r2, 1)]

        for rt in range(4):
            r = 4 * st + rt
            cls = chunks_of(r)
            n = len(cls)
            for hh in range(H):
                i = att_i(r, hh)
                sslot = i % 2
                eslot = i % 2

                def sc_fn(e, r=r, hh=hh, cls=cls, sslot=sslot, sl=sl):
                    last = None
                    for ci, c in enumerate(cls):
                        b0 = 2 * sslot + ci // 4
                        blk = ci % 4
                        dst = bank(b0, 128 * blk, 128 * blk + 128)
                        last = e.matmul(dst, s2_kr[:, c % 18, hh, :],
                                        s2_q[:, sl, hh, 128 * (r % 4):128 * (r % 4) + 128],
                                        start=True, stop=False, skip_group_check=True)
                        last = e.matmul(dst, s2_mlog[:, ci, :], s_id[:],
                                        start=False, stop=True, skip_group_check=True)
                    return last
                wts = [ch_load[r][0], ch_load[r][1], q2l[st]]
                if i >= 2:
                    wts.append(("act", exp_done[i - 2]))
                if hh < 2:
                    if rt == 0 and st > 0:
                        wts.append(("dve", h_done[st - 1]))
                        wts.append(("act", wo_done[st - 1]))
                W("tensor", sc_fn, waits=wts, inc="pe")
                sc_done[i] = pl.val["pe"]

                def ex_fn(e, n=n, sslot=sslot, eslot=eslot):
                    src = PS[:, 512 * (2 * sslot): 512 * (2 * sslot) + 128 * n]
                    return e.activation(s2_er[:, eslot, 0:128 * n], src, AF.Exp,
                                        bias=s_con[:, 0:1], scale=s_con[:, 1:2])
                wts = [("pe", sc_done[i])]
                if i >= 2:
                    wts.append(("pe", tele_done[i - 2]))
                W("scalar", ex_fn, waits=wts, inc="act")
                exp_done[i] = pl.val["act"]

                if pend[0] is not None:
                    emit_te(pend[0])
                pend[0] = (r, rt, hh, cls, n, eslot, i)

                if hh == 0 and r + 1 < NT:
                    fkl, fvl = chload_fns(r + 1)
                    stn = (r + 1) // 4
                    W("sync", fkl, waits=[("d_k", 16 * (stn + 1)),
                                          ("pe", sc_done[i]),
                                          ("pe", tele_done.get(att_i(r - 1, 7), 0))],
                      inc="d_chk", amt=16)
                    W("sync", fvl, waits=[("d_vt", 16 * (stn + 1)), ("pe", sc_done[i])],
                      inc="d_chv", amt=16)
                    ch_load[r + 1] = (("d_chk", pl.val["d_chk"]), ("d_chv", pl.val["d_chv"]))
        emit_te(pend[0])
        pend[0] = None

        hf = s2_h[:].rearrange("p h t -> p (h t)")
        W("vector", (lambda sl: lambda e: e.tensor_tensor(
            hf, hf, s2_cv[:, sl, :, :].rearrange("p h t -> p (h t)"), ALU.add))(sl),
            waits=[cv2l[st], ("dve", norm_done[4 * st + 3])], inc="dve")
        cadd_done = pl.val["dve"]

        lown = {}
        zev = {}
        zdone = [0, 0]
        for hp in range(4):
            h0, h1 = 2 * hp, 2 * hp + 1

            def z_fn(e, h0=h0, h1=h1):
                e.matmul(bank(0, 0, 512), s2_fkr[:, h0, :], s2_fq[:, h0, :],
                         start=True, stop=True, skip_group_check=True)
                return e.matmul(bank(1, 0, 512), s2_fkr[:, h1, :], s2_fq[:, h1, :],
                                start=True, stop=True, skip_group_check=True)
            wts = [("dve", fq_done), ("pe", tele_done[att_i(4 * st + 3, 7)])]
            if st == 0 and hp == 0:
                wts.append(("d_vt", 16 * NST))
            if hp >= 1:
                wts.append(("act", zev[hp - 1]))
            W("tensor", z_fn, waits=wts, inc="pe")
            zmm = pl.val["pe"]

            W("scalar", (lambda h0: lambda e: e.copy(s2_zb[:, h0, :], bank(0, 0, 512)))(h0),
              waits=[("pe", zmm)], inc="act")
            W("scalar", (lambda h1: lambda e: e.copy(s2_zb[:, h1, :], bank(1, 0, 512)))(h1),
              waits=[("pe", zmm)], inc="act")
            zev[hp] = pl.val["act"]

            if hp % 2 == 1:
                half = hp // 2
                zbf = s2_zb[:, 4 * half:4 * half + 4, :].rearrange("p s t -> p (s t)")
                W("scalar", (lambda zbf: lambda e: e.activation(zbf, zbf, AF.Ln, bias=s_con[:, 2:3]))(zbf),
                  waits=[("act", zev[hp])], inc="act")
                lnv = pl.val["act"]
                W("scalar", (lambda zbf: lambda e: e.activation(zbf, zbf, AF.Exp, scale=s_con[:, 4:5]))(zbf),
                  waits=[("act", lnv)], inc="act")
                zdone[half] = pl.val["act"]

        LBNK = (4, 5, 0, 1)
        for hh in range(H):
            zb_ = LBNK[hh % 4]

            def lmm_fn(e, hh=hh, zb_=zb_):
                return e.matmul(bank(zb_, 0, 512), s2_kvb[:, hh, :], s2_fq[:, hh, :],
                                start=True, stop=True, skip_group_check=True)
            lw = [("dve", lown[hh - 4]) if hh >= 4 else ("dve", norm_done[4 * st + 3])]
            if 2 <= hh < 4:
                lw.append(("act", zev[3]))
            W("tensor", lmm_fn, waits=lw, inc="pe")
            lmm = pl.val["pe"]

            def lno_fn(e, hh=hh, zb_=zb_):
                cinc(e.tensor_tensor(s2_scr[:], bank(zb_, 0, 512), s2_zb[:, hh, :], ALU.mult))
                cwt(e)
                return e.tensor_tensor(s2_h[:, hh, :], s2_h[:, hh, :], s2_scr[:], ALU.add)
            W("vector", lno_fn, waits=[("pe", lmm), ("act", zdone[hh // 4]), ("dve", cadd_done),
                                       ("dve", lown.get(hh - 1, 0))], inc="dve")
            lown[hh] = pl.val["dve"]
        h_done[st] = lown[7]

        woe = {}
        for mp in range(4):
            m0, m1 = 2 * mp, 2 * mp + 1

            lw = tuple(lown[kc] for kc in range(8))

            def wmm_fn(e, m0=m0, m1=m1, lw=lw):
                last = None
                for kc in range(8):
                    e.wait_ge(pl.sem["dve"], lw[kc])
                    e.matmul(bank(2, 0, 512), s2_wo[:, kc, 128 * m0:128 * (m0 + 1)],
                             s2_h[:, kc, :], start=(kc == 0), stop=(kc == 7),
                             skip_group_check=True)
                    last = e.matmul(bank(3, 0, 512), s2_wo[:, kc, 128 * m1:128 * (m1 + 1)],
                                    s2_h[:, kc, :], start=(kc == 0), stop=(kc == 7),
                                    skip_group_check=True)
                return last
            wts = [("d_w", wo_loaded)]
            if mp >= 1:
                wts.append(("act", woe[m1 - 2]))
            W("tensor", wmm_fn, waits=wts, inc="pe")
            wmm = pl.val["pe"]
            W("scalar", (lambda m0: lambda e: e.copy(s2_wos[:, m0, :], bank(2, 0, 512)))(m0),
              waits=[("pe", wmm)], inc="act")
            W("scalar", (lambda m1: lambda e: e.copy(s2_wos[:, m1, :], bank(3, 0, 512)))(m1),
              waits=[("pe", wmm)], inc="act")
            woe[m1] = pl.val["act"]
        wo_done[st] = woe[7]

        for rt in range(4):
            r = 4 * st + rt
            oslot = rt % 2

            def oT_fn(e, rt=rt, oslot=oslot):
                last = None
                for m in range(8):
                    last = e.transpose(
                        PSb[:, 1024 * (4 + oslot) + 128 * m: 1024 * (4 + oslot) + 128 * m + 128],
                        s2_wos[:, m, 128 * rt:128 * rt + 128], s_id[:])
                return last
            wts = [("act", wo_done[st])]
            if r >= 2:
                wts.append(("dve", out_done[r - 2]))
            if rt == 0:
                wts.append(("dve", norm_done[4 * st + 3]))
            W("tensor", oT_fn, waits=wts, inc="pe")
            otd = pl.val["pe"]

            def oadd_fn(e, rt=rt, oslot=oslot):
                return e.tensor_tensor(s2_p[:, rt, :], s2_p[:, rt, :],
                                       PSb[:, 1024 * (4 + oslot): 1024 * (4 + oslot) + 1024], ALU.add)
            W("vector", oadd_fn, waits=[("pe", otd), p2l[r]], inc="dve")
            out_done[r] = pl.val["dve"]

            W("sync", (lambda r, rt: lambda e: e.dma_start(
                out=y_d[128 * r:128 * (r + 1), :], in_=s2_p[:, rt, :]))(r, rt),
                waits=[("dve", out_done[r])], inc=("d_y0", "d_y1", "d_y2", "d_y3")[rt], amt=16)

    W("sync", lambda e: None, waits=[("d_y0", 16 * NST), ("d_y1", 16 * NST),
                                     ("d_y2", 16 * NST), ("d_y3", 16 * NST)])

    with nc.Block() as blk:
        blk.sync(lambda e: pl.replay("sync", e))
        blk.gpsimd(lambda e: pl.replay("gpsimd", e))
        blk.tensor(lambda e: pl.replay("tensor", e))
        blk.scalar(lambda e: pl.replay("scalar", e))
        blk.vector(lambda e: pl.replay("vector", e))

    stck.close()
    return nc, sbuf_used


_NC_CACHE = {}


def kernel(L0_attn_x, L0_attn_Wqkv, L0_attn_Wo, L0_attn_conv_w,
           L0_ffn_W_r, L0_ffn_W_k, L0_ffn_W_v, L0_ffn_mu_r, L0_ffn_mu_k):
    x = np.ascontiguousarray(np.asarray(L0_attn_x, dtype=np.float32))
    wqkv = np.ascontiguousarray(np.asarray(L0_attn_Wqkv, dtype=np.float32))
    wo = np.ascontiguousarray(np.asarray(L0_attn_Wo, dtype=np.float32))
    cw = np.asarray(L0_attn_conv_w, dtype=np.float32)
    wrt = np.ascontiguousarray(np.asarray(L0_ffn_W_r, dtype=np.float32).T)
    wkt = np.ascontiguousarray(np.asarray(L0_ffn_W_k, dtype=np.float32).T)
    wvt = np.ascontiguousarray(np.asarray(L0_ffn_W_v, dtype=np.float32).T)
    mur = np.ascontiguousarray(np.asarray(L0_ffn_mu_r, dtype=np.float32).reshape(H, 128).T)
    muk = np.ascontiguousarray(np.asarray(L0_ffn_mu_k, dtype=np.float32).reshape(H, 128).T)
    cwl = np.ascontiguousarray(cw.reshape(H, 128, 7).transpose(1, 0, 2))

    if "nc" not in _NC_CACHE:
        _NC_CACHE["nc"] = build_nc()
    nc, _ = _NC_CACHE["nc"]

    base = {"wqkv": wqkv, "wo": wo, "wrt": wrt, "wkt": wkt, "wvt": wvt,
            "cw": cwl, "mur": mur, "muk": muk}
    in_maps = [dict(base, x=np.ascontiguousarray(x[b])) for b in range(8)]
    res = run_bass_kernel_spmd(nc, in_maps, core_ids=list(range(8)))
    return np.stack([res.results[b]["y"] for b in range(8)], axis=0)


if __name__ == "__main__":
    nc, used = build_nc()
    print("sbuf bytes/partition used:", used, " room-224k:", 224 * 1024 - used)
    print("instructions:", nc.n_instructions())


# version 27
# speedup vs baseline: 1.0065x; 1.0065x over previous
import numpy as np
import ml_dtypes
import concourse.bass as bass
import concourse.mybir as mybir
from concourse.bass_utils import run_bass_kernel_spmd

F32 = mybir.dt.float32
BF16 = mybir.dt.bfloat16
AF = mybir.ActivationFunctionType
ALU = mybir.AluOpType
BF = ml_dtypes.bfloat16

S, C, H, D = 4096, 1024, 8, 128
NT, NST = 32, 8
CB = 12.0
SCL = 1.0 / np.sqrt(128.0)
LAGS = (0, 1, 2, 4, 8, 16)


def chunks_of(r):
    return [r - m for m in LAGS if r - m >= 0]


def mlog_np():
    m = masks_np()
    ml = np.where(m > 0, 0.0, -512.0).astype(np.float32)
    return np.ascontiguousarray(ml.transpose(2, 0, 1))


def masks_np():
    m = np.zeros((6, 128, 128), np.float32)
    for o in (0, 1, 2, 4, 8, 16, 32, 64):
        for i in range(128):
            p = i - o
            if p >= 0:
                m[0, p, i] = 1.0
            else:
                m[1, p + 128, i] = 1.0
    m[1] += np.eye(128, dtype=np.float32)
    for s in range(2, 6):
        m[s] = np.eye(128)
    return m


class Plan:
    def __init__(self, nc):
        self.nc = nc
        self.items = {"sync": [], "gpsimd": [], "tensor": [], "scalar": [], "vector": []}
        self.sem = {}
        self.val = {}

    def add_sem(self, name, stack):
        self.sem[name] = stack.enter_context(self.nc.semaphore(name))
        self.val[name] = 0

    def step(self, eng, fn, waits=(), inc=None, amt=1):
        if inc is not None:
            self.val[inc] += amt
        self.items[eng].append((fn, tuple(waits), inc, amt))
        return self.val[inc] if inc is not None else None

    def replay(self, eng_name, eng):
        for fn, waits, inc, amt in self.items[eng_name]:
            for s, v in waits:
                if v > 0:
                    eng.wait_ge(self.sem[s], v)
            last = fn(eng)
            if inc is not None:
                assert last is not None
                last.then_inc(self.sem[inc], amt)


def build_nc(mode="full", upto=99, nst_lim=99):
    nc = bass.Bass("TRN2", target_bir_lowering=False)

    x_d = nc.dram_tensor("x", [S, C], F32, kind="ExternalInput").ap()
    wqkv_d = nc.dram_tensor("wqkv", [C, 3 * C], F32, kind="ExternalInput").ap()
    wo_d = nc.dram_tensor("wo", [C, C], F32, kind="ExternalInput").ap()
    wrt_d = nc.dram_tensor("wrt", [C, C], F32, kind="ExternalInput").ap()
    wkt_d = nc.dram_tensor("wkt", [C, C], F32, kind="ExternalInput").ap()
    wvt_d = nc.dram_tensor("wvt", [C, C], F32, kind="ExternalInput").ap()
    cw_d = nc.dram_tensor("cw", [128, H, 7], F32, kind="ExternalInput").ap()
    mur_d = nc.dram_tensor("mur", [128, H], F32, kind="ExternalInput").ap()
    muk_d = nc.dram_tensor("muk", [128, H], F32, kind="ExternalInput").ap()
    y_d = nc.dram_tensor("y", [S, C], F32, kind="ExternalOutput").ap()

    q_sp = nc.dram_tensor("q_sp", [NST, 128, H, 512], BF16, kind="Internal").ap()
    k_sp = nc.dram_tensor("k_sp", [NST, 128, H, 512], BF16, kind="Internal").ap()
    vt_sp = nc.dram_tensor("vt_sp", [NST, 128, 4, H, 128], BF16, kind="Internal").ap()
    cv_sp = nc.dram_tensor("cv_sp", [NST, 128, H, 512], BF16, kind="Internal").ap()
    p_sp = nc.dram_tensor("p_sp", [NST, 128, 4, C], F32, kind="Internal").ap()

    id_t = nc.inline_tensor(np.eye(128, dtype=BF), "id_t")
    ones_t = nc.inline_tensor(np.ones((128, 128), dtype=BF), "ones_t")
    mlog_t = nc.inline_tensor(mlog_np().astype(BF), "mlog_t")
    cvals = np.zeros((128, 8), np.float32)
    cvals[:, 0], cvals[:, 1], cvals[:, 2], cvals[:, 3] = -CB, SCL, 1e-6, 1e-5
    cvals[:, 4] = -1.0
    con_t = nc.inline_tensor(cvals, "con_t")

    import contextlib
    stck = contextlib.ExitStack()
    SBASE = ((nc.sbuf_base + 31) // 32) * 32

    def mk_region(base):
        state = {"off": base}

        def alloc(name, shape, dtype):
            esz = 2 if dtype == BF16 else 4
            per = int(np.prod(shape[1:])) * esz
            off = state["off"]
            t = nc.alloc_sbuf_tensor_at(name, list(shape), dtype, offset=off)
            state["off"] = off + ((per + 31) // 32) * 32
            return t
        return alloc, state

    ap_, pstate = mk_region(SBASE)
    s_id = ap_("s_id", [128, 128], BF16)
    s_ones = ap_("s_ones", [128, 128], BF16)
    s_con = ap_("s_con", [128, 8], F32)
    s_cw = ap_("s_cw", [128, H, 7], F32)
    s_mur = ap_("s_mur", [128, H], F32)
    s_muk = ap_("s_muk", [128, H], F32)
    s_omr = ap_("s_omr", [128, H], F32)
    s_omk = ap_("s_omk", [128, H], F32)
    s_kvsb = ap_("s_kvsb", [128, H, 129], F32)
    carry_v = ap_("carry_v", [128, H, 6], BF16)
    carry_xn = ap_("carry_xn", [128, H, 1], BF16)
    s2_mlog = ap_("s2_mlog", [128, 6, 128], BF16)
    s2_fkr = ap_("s2_fkr", [128, H, 128], BF16)
    s2_kvb = ap_("s2_kvb", [128, H, 128], BF16)
    base_ov = pstate["off"]

    a1, st1 = mk_region(base_ov)
    s1_wq = a1("s1_wq", [128, 8, 3072], BF16)
    s1_wr = a1("s1_wr", [128, 8, C], BF16)
    s1_wk = a1("s1_wk", [128, 8, C], BF16)
    s1_wv = a1("s1_wv", [128, 8, C], BF16)
    s1_xa = a1("s1_xa", [128, 2, C], F32)
    s1_xb = a1("s1_xb", [128, C], F32)
    s1_xnb = a1("s1_xnb", [128, 4, C], BF16)
    s1_xnfm = a1("s1_xnfm", [128, 8, 513], BF16)
    qk_off = st1["off"]
    s1_qk = a1("s1_qk", [128, H, 512], BF16)
    s1_xb2 = nc.alloc_sbuf_tensor_at("s1_xb2", [128, C], F32, offset=qk_off)
    s1_fk = a1("s1_fk", [128, H, 512], BF16)
    s1_xr = a1("s1_xr", [128, H, 512], BF16)
    s1_rg = a1("s1_rg", [128, H, 512], BF16)
    s1_vh = a1("s1_vh", [128, H, 518], BF16)
    s1_vt = a1("s1_vt", [128, 4, H, 129], BF16)
    s1_fkt = a1("s1_fkt", [128, 4, H, 128], BF16)
    s1_cv = a1("s1_cv", [128, H, 512], BF16)
    s1_p = a1("s1_p", [128, 2, C], F32)
    s1_st = a1("s1_st", [128, 24], F32)
    end1 = st1["off"]

    a2, st2 = mk_region(base_ov)
    s2_kr = a2("s2_kr", [128, 18, H, 128], BF16)
    s2_vr = a2("s2_vr", [128, 18, H, 128], BF16)
    s2_wo = a2("s2_wo", [128, 8, C], BF16)
    s2_q = a2("s2_q", [128, 2, H, 512], BF16)
    s2_cv = a2("s2_cv", [128, 2, H, 512], BF16)
    s2_fq = a2("s2_fq", [128, H, 512], BF16)
    s2_fqs = a2("s2_fqs", [128, H, 512], BF16)
    s2_er = a2("s2_er", [128, 2, 768], BF16)
    s2_em = a2("s2_em", [128, 2, 768], BF16)
    s2_h = a2("s2_h", [128, H, 512], BF16)
    s2_rr = a2("s2_rr", [128, 1024], F32)
    s2_zb = a2("s2_zb", [128, 8, 512], F32)
    s2_scr = a2("s2_scr", [128, 512], BF16)
    s2_wos = a2("s2_wos", [128, 8, 512], BF16)
    s2_p = a2("s2_p", [128, 4, C], F32)
    end2 = st2["off"]
    sbuf_used = max(end1, end2)
    got = nc.bump_sbuf(sbuf_used - SBASE + 32)
    assert got is not None, f"SBUF over budget: used {sbuf_used}"

    PS = stck.enter_context(nc.psum_tensor("PS", [128, 4096], F32))
    PSb = PS[:].bitcast(BF16)

    def bank(b, c0, c1):
        return PS[:, 512 * b + c0: 512 * b + c1]

    pl = Plan(nc)
    for sname in ["d_w", "d_wc", "d_x0", "d_x1", "d_xb0", "d_xb1", "d_q", "d_k",
                  "d_vt", "d_cv", "d_p", "d_q2a", "d_q2b", "d_cv2a", "d_cv2b",
                  "d_p2a", "d_p2b", "d_chk", "d_chv", "d_y0", "d_y1", "d_y2", "d_y3",
                  "pe", "act", "dve", "sq", "pool"]:
        pl.add_sem(sname, stck)
    W = pl.step
    rtc = {"sq": 0}

    def cinc(inst):
        inst.then_inc(pl.sem["sq"], 1)
        rtc["sq"] += 1
        return inst

    def cwt(e):
        e.wait_ge(pl.sem["sq"], rtc["sq"])

    rtp = {"n": 0}

    def pinc(inst):
        inst.then_inc(pl.sem["pool"], 1)
        rtp["n"] += 1
        return inst

    def pwt(e):
        e.wait_ge(pl.sem["pool"], rtp["n"])

    W("sync", lambda e: e.dma_start(out=s_id[:], in_=id_t.ap()), inc="d_wc", amt=16)
    W("sync", lambda e: e.dma_start(out=s_ones[:], in_=ones_t.ap()), inc="d_wc", amt=16)
    W("sync", lambda e: e.dma_start(out=s_con[:], in_=con_t.ap()), inc="d_wc", amt=16)
    W("sync", lambda e: e.dma_start(out=s2_mlog[:], in_=mlog_t.ap()), inc="d_wc", amt=16)
    W("sync", lambda e: e.dma_start(out=s_cw[:], in_=cw_d), inc="d_wc", amt=16)
    W("sync", lambda e: e.dma_start(out=s_mur[:], in_=mur_d), inc="d_wc", amt=16)
    W("sync", lambda e: e.dma_start(out=s_muk[:], in_=muk_d), inc="d_wc", amt=16)
    W("gpsimd", lambda e: e.dma_start(out=s1_wq[:], in_=wqkv_d.rearrange("(k p) m -> p k m", p=128)), inc="d_w", amt=16)
    W("gpsimd", lambda e: e.dma_start(out=s1_wr[:], in_=wrt_d.rearrange("(k p) m -> p k m", p=128)),
      waits=[("d_w", 16)], inc="d_w", amt=16)
    W("gpsimd", lambda e: e.dma_start(out=s1_wk[:], in_=wkt_d.rearrange("(k p) m -> p k m", p=128)),
      waits=[("d_w", 32)], inc="d_w", amt=16)
    W("gpsimd", lambda e: e.dma_start(out=s1_wv[:], in_=wvt_d.rearrange("(k p) m -> p k m", p=128)),
      waits=[("d_w", 48)], inc="d_w", amt=16)
    wload = pl.val["d_w"]
    wcload = pl.val["d_wc"]

    def init_fn(e):
        e.tensor_scalar(s_omr[:], s_mur[:], -1.0, 1.0, ALU.mult, ALU.add)
        e.tensor_scalar(s_omk[:], s_muk[:], -1.0, 1.0, ALU.mult, ALU.add)
        e.memset(s_kvsb[:], 0.0)
        e.memset(carry_v[:], 0.0)
        e.memset(carry_xn[:], 0.0)
        e.memset(s1_vh[:, :, 0:6], 0.0)
        e.memset(s1_xnfm[:, :, 0:1], 0.0)
        return e.memset(s1_vt[:], 1.0)
    W("vector", init_fn, waits=[("d_w", wload), ("d_wc", wcload)], inc="dve")
    init_done = pl.val["dve"]

    ln_done = [0] * NT
    lnf_done = [0] * NT
    xnT_evac = [0] * NT
    xnT_pe = [0] * NT
    qkv_evac = {}
    qkv_pe = [0] * NST
    rmm_pe = [0] * NST
    kmm_pe = [0] * NST
    vmm_pe = [0] * NST
    trans_pe = [0] * NST
    vT_pe = [0] * NST
    kv_pe = [0] * NST
    convh_done = [[0] * H for _ in range(NST)]
    carry_ci = [0] * NST
    fk_scr_done = [0] * NST
    kve_done = [0] * NST
    vt_done = [0] * NST
    ffn_tt_last = [(0, 0, 0, 0)] * NST
    BNK4 = (0, 1, 2, 7)
    pas_last = [0] * NST
    p_spilled = [0] * NST
    carry_in = [0] * NST
    xk_view = s1_fkt[:].rearrange("p r h t -> p (r h t)").rearrange("p (h t) -> p h t", h=H)
    pre = {}

    def emit_carry_in(st):
        def ci_fn(e):
            for ch in range(H - 1):
                e.tensor_copy(s1_vh[:, ch, 0:6], carry_v[:, ch, :])
            return e.tensor_copy(s1_vh[:, H - 1, 0:6], carry_v[:, H - 1, :])
        W("vector", ci_fn,
          waits=[("act", qkv_evac[(st, 23)]),
                 ("dve", convh_done[st - 1][7] if st > 0 else 0)], inc="dve")
        carry_ci[st] = pl.val["dve"]

    def emit_conv_head(st, ch, extra=()):
        def conv_fn(e, st=st, ch=ch):
            for j in range(7):
                src = s1_vh[:, ch, j:512 + j]
                if j == 0:
                    cinc(e.tensor_scalar(s1_cv[:, ch, :], src, s_cw[:, ch, j:j + 1], None, ALU.mult))
                else:
                    cinc(e.scalar_tensor_tensor(s1_cv[:, ch, :], src, s_cw[:, ch, j:j + 1],
                                                s1_cv[:, ch, :], ALU.mult, ALU.add))
                cwt(e)
            return e.tensor_copy(carry_v[:, ch, :], s1_vh[:, ch, 512:518])
        wts = [("act", qkv_evac[(st, 16 + ch)]), ("d_cv", 16 * st),
               ("dve", carry_ci[st])] + list(extra)
        W("vector", conv_fn, waits=wts, inc="dve")
        convh_done[st][ch] = pl.val["dve"]

    def emit_conv_head_act(st, ch, ft_pe):
        ujd = {}
        addd = {}

        def t0_fn(e, st=st, ch=ch):
            return e.tensor_scalar(s1_cv[:, ch, :], s1_vh[:, ch, 0:512],
                                   s_cw[:, ch, 0:1], None, ALU.mult)
        W("vector", t0_fn,
          waits=[("act", qkv_evac[(st, 16 + ch)]), ("d_cv", 16 * st),
                 ("dve", carry_ci[st])], inc="dve")
        addd[0] = pl.val["dve"]
        for j in range(1, 7):
            wts = [("dve", addd[j - 2] if j >= 2 else 0), ("pe", ft_pe)]
            if j <= 2 and ch > 4:
                wts.append(("dve", convh_done[st][ch - 1]))
            W("scalar", (lambda st, ch, j: lambda e: e.activation(
                s1_rg[:, j % 2, :], s1_vh[:, ch, j:512 + j], AF.Identity,
                scale=s_cw[:, ch, j:j + 1]))(st, ch, j), waits=wts, inc="act")
            ujd[j] = pl.val["act"]

            def add_fn(e, st=st, ch=ch, j=j):
                if j < 6:
                    return e.tensor_tensor(s1_cv[:, ch, :], s1_cv[:, ch, :],
                                           s1_rg[:, j % 2, :], ALU.add)
                cinc(e.tensor_tensor(s1_cv[:, ch, :], s1_cv[:, ch, :],
                                     s1_rg[:, j % 2, :], ALU.add))
                cwt(e)
                return e.tensor_copy(carry_v[:, ch, :], s1_vh[:, ch, 512:518])
            W("vector", add_fn, waits=[("act", ujd[j]), ("dve", addd[j - 1])], inc="dve")
            addd[j] = pl.val["dve"]
        convh_done[st][ch] = pl.val["dve"]

    def emit_ln_a(st):
        pp = st % 2
        vars_ = [0] * 4
        for rt in range(4):
            r = 4 * st + rt
            sl = rt % 2
            xs = "d_x0" if rt % 2 == 0 else "d_x1"
            W("sync", (lambda sl, r: lambda e: e.dma_start(
                out=s1_xa[:, sl, :], in_=x_d[128 * r:128 * (r + 1), :]))(sl, r),
                waits=[("dve", lnf_done[r - 2] if r >= 2 else 0)], inc=xs, amt=16)
            xw = (xs, pl.val[xs])

            def ln_fn(e, sl=sl, rt=rt, pp=pp):
                xin = s1_xa[:, sl, :]
                cinc(e.scalar_tensor_tensor(s1_xnb[:, rt, :], xin, 1.0, xin, ALU.mult, ALU.mult, accum_out=s1_st[:, 1:2]))
                cwt(e)
                cinc(e.tensor_scalar(s1_xnb[:, rt, :], xin, 1.0, None, ALU.mult, ALU.add, accum_out=s1_st[:, 0:1]))
                cwt(e)
                mu = s1_st[:, 8 + 4 * pp + rt:9 + 4 * pp + rt]
                cinc(e.tensor_scalar(mu, s1_st[:, 0:1], 1.0 / C, None, ALU.mult))
                e.tensor_scalar(s1_st[:, 2:3], s1_st[:, 1:2], 1.0 / C, None, ALU.mult)
                cwt(e)
                cinc(e.tensor_scalar(s1_st[:, 3:4], mu, mu, None, ALU.mult))
                cwt(e)
                return e.tensor_tensor(s1_st[:, 4 + rt:5 + rt], s1_st[:, 2:3], s1_st[:, 3:4], ALU.subtract)
            W("vector", ln_fn, waits=[xw, ("dve", init_done),
                                      ("pe", xnT_pe[r - 4] if r >= 4 else 0)], inc="dve")
            vars_[rt] = pl.val["dve"]
            lnf_done[r] = pl.val["dve"]
        return vars_

    def emit_ln_sqrt(st, vars_):
        stds = [0] * 4
        for rt in range(4):
            W("scalar", (lambda rt: lambda e: e.activation(
                s1_st[:, 4 + rt:5 + rt],
                s1_st[:, 4 + rt:5 + rt], AF.Sqrt, bias=s_con[:, 3:4]))(rt),
                waits=[("dve", vars_[rt])], inc="act")
            stds[rt] = pl.val["act"]
        return stds

    def emit_ln_norm(st, stds):
        pp = st % 2
        for rt in range(4):
            r = 4 * st + rt
            sl = rt % 2

            def ln2_fn(e, sl=sl, rt=rt, pp=pp):
                mu = s1_st[:, 8 + 4 * pp + rt:9 + 4 * pp + rt]
                rs = s1_st[:, 16 + 4 * pp + rt:17 + 4 * pp + rt]
                cinc(e.reciprocal(rs, s1_st[:, 4 + rt:5 + rt]))
                cwt(e)
                cinc(e.tensor_scalar(s1_xnb[:, rt, :], s1_xnb[:, rt, :], mu, rs,
                                     ALU.subtract, ALU.mult))
                cinc(e.tensor_tensor(mu, mu, rs, ALU.mult))
                cwt(e)
                return e.tensor_scalar(rs, rs, 1.0, None, ALU.add)
            W("vector", ln2_fn, waits=[("act", stds[rt])], inc="dve")
            ln_done[r] = pl.val["dve"]

    def emit_xnt(st, after_qkv):
        for rt in range(4):
            r = 4 * st + rt
            for half in range(2):
                xbk = 5 if half == 0 else 6

                def xnT_fn(e, rt=rt, xbk=xbk, half=half):
                    last = None
                    for cc in range(4):
                        c = 4 * half + cc
                        last = e.transpose(
                            PSb[:, 1024 * xbk + 128 * cc: 1024 * xbk + 128 * cc + 128],
                            s1_xnb[:, rt, 128 * c:128 * (c + 1)], s_id[:])
                    return last
                prev = xnT_evac[r - 1] if r >= 1 else 0
                W("tensor", xnT_fn, waits=[("dve", ln_done[r]), ("act", prev),
                                           ("dve", pas_last[st - 2] if st >= 2 else 0)], inc="pe")
                tdone = pl.val["pe"]

                def xnTe_fn(e, rt=rt, xbk=xbk, half=half):
                    return e.copy(
                        s1_xnfm[:, 4 * half:4 * half + 4, 1 + 128 * rt: 1 + 128 * rt + 128],
                        PSb[:, 1024 * xbk: 1024 * xbk + 512].rearrange("p (c t) -> p c t", c=4))
                wts = [("pe", tdone)]
                if rt == 0 and half == 0 and st > 0:
                    wts.append(("pe", after_qkv))
                W("scalar", xnTe_fn, waits=wts, inc="act")
            xnT_pe[r] = pl.val["pe"]
            xnT_evac[r] = pl.val["act"]
        W("vector", lambda e: e.tensor_copy(s1_xnfm[:, :, 0:1], carry_xn[:]),
          waits=[("act", xnT_evac[4 * st + 3])], inc="dve")
        carry_in[st] = pl.val["dve"]

    vars0 = emit_ln_a(0)
    stds0 = emit_ln_sqrt(0, vars0)
    emit_ln_norm(0, stds0)
    emit_xnt(0, 0)
    QORD = list(range(16, 24)) + list(range(0, 8)) + list(range(8, 16))
    for st in range(NST):
        if st >= nst_lim:
            continue
        carry_in_done = carry_in[st]

        if st + 1 < NST and st + 1 < nst_lim:
            nvars = emit_ln_a(st + 1)

        def d_fn(e):
            return e.tensor_tensor(s1_xr[:], s1_xnfm[:, :, 1:513],
                                   s1_xnfm[:, :, 0:512], ALU.subtract)
        W("vector", d_fn,
          waits=[("dve", carry_in_done), ("pe", vmm_pe[st - 1] if st > 0 else 0)], inc="dve")
        d_done = pl.val["dve"]

        def fxk_fn(e):
            last = None
            for ch in range(H):
                last = e.scalar_tensor_tensor(xk_view[:, ch, :], s1_xr[:, ch, :], s_muk[:, ch:ch + 1],
                                              s1_xnfm[:, ch, 0:512], ALU.mult, ALU.add)
            return last
        W("vector", fxk_fn,
          waits=[("dve", d_done), ("pe", kv_pe[st - 1] if st > 0 else 0)], inc="dve")
        xk_done = pl.val["dve"]

        def fxr_fn(e):
            last = None
            for ch in range(H):
                last = e.scalar_tensor_tensor(s1_xr[:, ch, :], s1_xr[:, ch, :], s_mur[:, ch:ch + 1],
                                              s1_xnfm[:, ch, 0:512], ALU.mult, ALU.add)
            return last
        W("vector", fxr_fn, waits=[("dve", xk_done)], inc="dve")
        xr_done = pl.val["dve"]

        W("vector", lambda e: e.tensor_copy(carry_xn[:], s1_xnfm[:, :, 512:513]),
          waits=[("dve", xr_done)], inc="dve")

        fkmin = {}
        fkexp = {}
        fkadd = {}
        for i, m in enumerate(QORD):
            bsel = BNK4[i % 4]

            def qkv_fn(e, m=m, bsel=bsel):
                last = None
                for kc in range(8):
                    last = e.matmul(bank(bsel, 0, 512), s1_wq[:, kc, 128 * m:128 * (m + 1)],
                                    s1_xnfm[:, kc, 1:513], start=(kc == 0), stop=(kc == 7),
                                    skip_group_check=True)
                return last
            waits = [("dve", carry_in_done), ("d_w", wload)]
            if i >= 4:
                waits.append(("act", qkv_evac[(st, QORD[i - 4])]))
            elif st > 0:
                waits.append(("dve", ffn_tt_last[st - 1][i]))
            W("tensor", qkv_fn, waits=waits, inc="pe")
            mm_done = pl.val["pe"]

            if m >= 16:
                dst = s1_vh[:, m - 16, 6:518]
                extra = [("dve", convh_done[st - 1][m - 16] if st > 0 else 0)]
            elif m < 8:
                dst = s1_qk[:, m, :]
                extra = [("d_q", 16 * st), ("d_k", 16 * st),
                         ("dve", fk_scr_done[st - 1] if st > 0 else 0),
                         ("dve", pas_last[st - 1] if st > 0 else 0)]
            else:
                dst = s1_qk[:, m - 8, :]
                extra = [("d_q", 16 * (st + 1))]
            W("scalar", (lambda dst, bsel: lambda e: e.copy(dst, bank(bsel, 0, 512)))(dst, bsel),
              waits=[("pe", mm_done)] + extra, inc="act")
            qkv_evac[(st, m)] = pl.val["act"]

            if m == 4 and st + 1 < NST and st + 1 < nst_lim:
                nstds = emit_ln_sqrt(st + 1, nvars)
            if m == 7:
                W("gpsimd", (lambda st: lambda e: e.dma_start(out=q_sp[st], in_=s1_qk[:]))(st),
                  waits=[("act", qkv_evac[(st, 7)]), ("d_q", 16 * st)], inc="d_q", amt=16)
            if 8 <= m < 16:
                ch = m - 8
                wts = [("act", qkv_evac[(st, m)])]
                if ch == 0 and st > 0:
                    wts.append(("pe", trans_pe[st - 1]))
                W("scalar", (lambda ch: lambda e: e.activation(
                    s1_fk[:, ch, :], s1_qk[:, ch, :], AF.Exp))(ch),
                    waits=wts, inc="act")
                fkexp[ch] = pl.val["act"]

                def fka_fn(e, ch=ch):
                    cinc(e.tensor_scalar(s1_rg[:, 6 + ch % 2, :], s1_qk[:, ch, :], 0.0, None, ALU.max))
                    cwt(e)
                    return e.scalar_tensor_tensor(s1_fk[:, ch, :], s1_fk[:, ch, :], 1.0,
                                                  s1_rg[:, 6 + ch % 2, :], ALU.min, ALU.add)
                wts = [("act", fkexp[ch])]
                if ch >= 2:
                    wts.append(("dve", fkadd[ch - 2]))
                W("vector", fka_fn, waits=wts, inc="dve")
                fkadd[ch] = pl.val["dve"]
                if m == 9 and st + 1 < NST and st + 1 < nst_lim:
                    emit_ln_norm(st + 1, nstds)
            if m == 15:
                W("gpsimd", (lambda st: lambda e: e.dma_start(out=k_sp[st], in_=s1_qk[:]))(st),
                  waits=[("act", qkv_evac[(st, 15)]), ("d_k", 16 * st)], inc="d_k", amt=16)
        qkv_pe[st] = pl.val["pe"]
        fk_scr_done[st] = fkadd[7]

        emit_carry_in(st)

        if st + 1 < NST and st + 1 < nst_lim:
            emit_xnt(st + 1, qkv_pe[st])

        emit_conv_head(st, 0)

        rg_evac = {}
        for m in range(8):
            bsel = BNK4[m % 4]

            def rmm_fn(e, m=m, bsel=bsel):
                last = None
                for kc in range(8):
                    last = e.matmul(bank(bsel, 0, 512), s1_wr[:, kc, 128 * m:128 * (m + 1)],
                                    s1_xr[:, kc, :], start=(kc == 0), stop=(kc == 7),
                                    skip_group_check=True)
                return last
            wts = [("dve", xr_done)]
            wts.append(("act", rg_evac[m - 4]) if m >= 4 else ("act", qkv_evac[(st, QORD[20 + m])]))
            W("tensor", rmm_fn, waits=wts, inc="pe")
            rmm = pl.val["pe"]
            sg_w = [("pe", rmm)]
            if m >= 4:
                sg_w.append(("dve", fk_scr_done[st]))
            W("scalar", (lambda m, bsel: lambda e: e.activation(
                s1_rg[:, m, :], bank(bsel, 0, 512), AF.Sigmoid))(m, bsel),
                waits=sg_w, inc="act")
            rg_evac[m] = pl.val["act"]
        rmm_pe[st] = pl.val["pe"]

        emit_conv_head(st, 1)

        ka_evac = {}
        mx_done = {}
        sq_done = {}
        for m in range(8):
            bsel = BNK4[m % 4]

            def kmm_fn(e, m=m, bsel=bsel):
                last = None
                for kc in range(8):
                    last = e.matmul(bank(bsel, 0, 512), s1_wk[:, kc, 128 * m:128 * (m + 1)],
                                    xk_view[:, kc, :], start=(kc == 0), stop=(kc == 7),
                                    skip_group_check=True)
                return last
            wts = [("dve", xk_done)]
            wts.append(("act", ka_evac[m - 4]) if m >= 4 else ("act", rg_evac[4 + m]))
            W("tensor", kmm_fn, waits=wts, inc="pe")
            kmm = pl.val["pe"]
            W("scalar", (lambda m, bsel: lambda e: e.activation(
                s1_xr[:, m, :], bank(bsel, 0, 512), AF.Relu))(m, bsel),
                waits=[("pe", kmm)], inc="act")
            ka_evac[m] = pl.val["act"]
            W("gpsimd", (lambda m: lambda e: e.tensor_tensor(
                s1_xr[:, m, :], s1_xr[:, m, :], s1_xr[:, m, :], ALU.mult))(m),
                waits=[("act", ka_evac[m])], inc="pool")
            sq_done[m] = pl.val["pool"]
            if m == 3:
                emit_conv_head(st, 2)
            if m == 7:
                emit_conv_head(st, 3)
        kmm_pe[st] = pl.val["pe"]

        ffn_tt = {}
        for m in range(8):
            bsel = BNK4[m % 4]

            def vmm_fn(e, m=m, bsel=bsel):
                last = None
                for kc in range(8):
                    last = e.matmul(bank(bsel, 0, 512), s1_wv[:, kc, 128 * m:128 * (m + 1)],
                                    s1_xr[:, kc, :], start=(kc == 0), stop=(kc == 7),
                                    skip_group_check=True)
                return last
            wts = [("pool", sq_done[7])]
            if m >= 4:
                wts.append(("dve", ffn_tt[m - 4]))
            else:
                wts.append(("act", ka_evac[4 + m]))
            W("tensor", vmm_fn, waits=wts, inc="pe")
            vmm = pl.val["pe"]
            W("vector", (lambda m, bsel: lambda e: e.tensor_tensor(
                s1_rg[:, m, :], s1_rg[:, m, :], bank(bsel, 0, 512), ALU.mult))(m, bsel),
                waits=[("pe", vmm)], inc="dve")
            ffn_tt[m] = pl.val["dve"]
        vmm_pe[st] = pl.val["pe"]
        ffn_tt_last[st] = (ffn_tt[4], ffn_tt[5], ffn_tt[6], ffn_tt[7])

        tr_evac = 0
        for which, src, dstbuf in (("v", s1_vh, s1_vt), ("f", s1_fk, s1_fkt)):
            for rt in range(4):
                slot = rt % 2

                def trT_fn(e, src=src, rt=rt, slot=slot, which=which):
                    last = None
                    for hh in range(8):
                        if which == "v":
                            inp = src[:, hh, 6 + 128 * rt: 6 + 128 * rt + 128]
                        else:
                            inp = src[:, hh, 128 * rt: 128 * rt + 128]
                        last = e.transpose(
                            PSb[:, 1024 * (5 + slot) + 128 * hh: 1024 * (5 + slot) + 128 * hh + 128],
                            inp, s_id[:])
                    return last
                wts = [("dve", fk_scr_done[st] if which == "f" else 0), ("act", tr_evac)]
                if which == "v" and rt == 0:
                    wts.append(("act", qkv_evac[(st, 23)]))
                    wts.append(("dve", pas_last[st - 1] if st > 0 else 0))
                    if st + 1 < NST:
                        wts.append(("act", xnT_evac[4 * (st + 1) + 3]))
                    wts.append(("dve", carry_ci[st]))
                W("tensor", trT_fn, waits=wts, inc="pe")
                t_done = pl.val["pe"]

                def trTe_fn(e, dstbuf=dstbuf, rt=rt, slot=slot, which=which):
                    dst = dstbuf[:, rt, :, 0:128] if which == "v" else dstbuf[:, rt, :, :]
                    return e.copy(
                        dst, PSb[:, 1024 * (5 + slot): 1024 * (5 + slot) + 1024].rearrange("p (h t) -> p h t", h=8))
                wts = [("pe", t_done)]
                if which == "v" and rt == 0:
                    wts.append(("d_vt", 16 * st))
                if which == "f" and rt == 0:
                    wts.append(("pe", kmm_pe[st]))
                W("scalar", trTe_fn, waits=wts, inc="act")
                tr_evac = pl.val["act"]
            if which == "v":
                vT_pe[st] = pl.val["pe"]
        trans_pe[st] = pl.val["pe"]
        vt_done[st] = tr_evac
        W("gpsimd", (lambda st: lambda e: e.dma_start(out=vt_sp[st], in_=s1_vt[:, :, :, 0:128]))(st),
          waits=[("act", vt_done[st]), ("d_vt", 16 * st)], inc="d_vt", amt=16)

        kvadd = {}
        for hh in range(H):
            bsel = 3 + (hh % 2)

            def kv_fn(e, hh=hh, bsel=bsel):
                last = None
                for rt in range(4):
                    last = e.matmul(bank(bsel, 0, 129), s1_fkt[:, rt, hh, :],
                                    s1_vt[:, rt, hh, 0:129], start=(rt == 0), stop=(rt == 3),
                                    skip_group_check=True)
                return last
            wts = [("act", vt_done[st])]
            if hh >= 2:
                wts.append(("dve", kvadd[hh - 2]))
            elif st > 0:
                wts.append(("dve", kve_done[st - 1]))
            else:
                wts.append(("dve", init_done))
            W("tensor", kv_fn, waits=wts, inc="pe")
            kvmm = pl.val["pe"]
            W("vector", (lambda hh, bsel: lambda e: e.tensor_tensor(
                s_kvsb[:, hh, :], s_kvsb[:, hh, :], bank(bsel, 0, 129), ALU.add))(hh, bsel),
                waits=[("pe", kvmm)], inc="dve")
            kvadd[hh] = pl.val["dve"]
        kve_done[st] = kvadd[7]
        kv_pe[st] = pl.val["pe"]

        if st == NST - 1 and mode == "full" and upto >= 99 and nst_lim >= 99:
            W("sync", lambda e: e.dma_start(out=s2_kr[:, 0, :, :], in_=k_sp[0, :, :, 0:128]),
              waits=[("d_k", 16), ("pe", qkv_pe[st])], inc="d_chk", amt=16)
            pre["ch0k"] = pl.val["d_chk"]
            W("sync", lambda e: e.dma_start(out=s2_vr[:, 0, :, :], in_=vt_sp[0, :, 0, :, :]),
              waits=[("d_vt", 16), ("pe", qkv_pe[st])], inc="d_chv", amt=16)
            pre["ch0v"] = pl.val["d_chv"]
            W("sync", lambda e: e.dma_start(out=s2_q[:, 0, :, :], in_=q_sp[0]),
              waits=[("d_q", 16), ("dve", lnf_done[NT - 1]), ("pe", vmm_pe[st])],
              inc="d_q2a", amt=16)
            pre["q2"] = ("d_q2a", pl.val["d_q2a"])

        pas_arr = [0] * 4
        pp = st % 2
        prev_pas = pas_last[st - 1] if st > 0 else 0
        for rt in range(4):
            g = 4 * st + rt
            slot = rt % 2
            xbuf = s1_xb if rt % 2 == 0 else s1_xb2

            def fT_fn(e, rt=rt, slot=slot):
                last = None
                for c in range(8):
                    last = e.transpose(
                        PSb[:, 1024 * (5 + slot) + 128 * c: 1024 * (5 + slot) + 128 * c + 128],
                        s1_rg[:, c, 128 * rt:128 * rt + 128], s_id[:])
                return last
            wts = [("dve", ffn_tt[7]), ("pe", kv_pe[st]),
                   ("dve", pas_arr[rt - 2] if rt >= 2 else prev_pas)]
            W("tensor", fT_fn, waits=wts, inc="pe")
            ftd = pl.val["pe"]

            lwts = [("d_xb0" if rt % 2 == 0 else "d_xb1", 16 * (g // 2))]
            lwts.append(("dve", pas_arr[rt - 2] if rt >= 2 else prev_pas))
            if rt == 1:
                lwts.append(("d_k", 16 * (st + 1)))
                lwts.append(("dve", fk_scr_done[st]))
            W("sync", (lambda rt, xbuf=xbuf, r=4 * st + rt: lambda e: e.dma_start(
                out=xbuf[:], in_=x_d[128 * r:128 * (r + 1), :]))(rt),
                waits=lwts, inc=("d_xb0" if rt % 2 == 0 else "d_xb1"), amt=16)
            xbl = (("d_xb0" if rt % 2 == 0 else "d_xb1"),
                   pl.val["d_xb0" if rt % 2 == 0 else "d_xb1"])

            def pas_fn(e, rt=rt, slot=slot, pp=pp, xbuf=xbuf):
                cinc(e.tensor_scalar(s1_p[:, rt % 2, :], xbuf[:],
                                     s1_st[:, 16 + 4 * pp + rt:17 + 4 * pp + rt],
                                     s1_st[:, 8 + 4 * pp + rt:9 + 4 * pp + rt],
                                     ALU.mult, ALU.subtract))
                cwt(e)
                return e.tensor_tensor(s1_p[:, rt % 2, :], s1_p[:, rt % 2, :],
                                       PSb[:, 1024 * (5 + slot): 1024 * (5 + slot) + 1024], ALU.add)
            W("vector", pas_fn,
              waits=[("pe", ftd), xbl,
                     ("d_p", 16 * (g - 1) if g >= 1 else 0)], inc="dve")
            pas_arr[rt] = pl.val["dve"]

            W("gpsimd", (lambda st, rt, g=g: lambda e: e.dma_start(
                out=p_sp[st, :, rt, :], in_=s1_p[:, rt % 2, :]))(st, rt),
                waits=[("dve", pas_arr[rt]), ("d_p", 16 * g)], inc="d_p", amt=16)
            if rt == 1:
                emit_conv_head(st, 4)
            if rt == 2:
                emit_conv_head(st, 5)
            if rt == 3:
                emit_conv_head(st, 6)
                emit_conv_head(st, 7)
        pas_last[st] = pas_arr[3]
        p_spilled[st] = pl.val["d_p"]
        W("gpsimd", (lambda st: lambda e: e.dma_start(out=cv_sp[st], in_=s1_cv[:]))(st),
          waits=[("dve", convh_done[st][7]), ("d_cv", 16 * st)], inc="d_cv", amt=16)


    sweep1_p = p_spilled[NST - 1]

    if upto < 99 or nst_lim < 99:
        for engn in pl.items:
            pl.items[engn] = [
                (fn, tuple((sn, min(v, pl.val[sn])) for sn, v in waits), inc, amt)
                for fn, waits, inc, amt in pl.items[engn]]
        drains = [(sn, pl.val[sn]) for sn in pl.val if pl.val[sn] > 0 and sn != "sq"]
        W("gpsimd", lambda e: e.dma_start(out=y_d[0:128, :], in_=s1_xa[:, 0, :]),
          waits=drains, inc="d_y0", amt=16)
        W("sync", lambda e: None, waits=[("d_y0", 16)])
        with nc.Block() as blk:
            blk.sync(lambda e: pl.replay("sync", e))
            blk.gpsimd(lambda e: pl.replay("gpsimd", e))
            blk.tensor(lambda e: pl.replay("tensor", e))
            blk.scalar(lambda e: pl.replay("scalar", e))
            blk.vector(lambda e: pl.replay("vector", e))
        stck.close()
        return nc, sbuf_used

    if mode == "sweep1":
        for st in range(NST):
            W("sync", (lambda st: lambda e: e.dma_start(
                out=y_d[512 * st:512 * (st + 1), :].rearrange("(r p) c -> p r c", p=128),
                in_=p_sp[st]))(st),
                waits=[("d_p", p_spilled[st]), ("d_y0", 16 * st)], inc="d_y0", amt=16)
        W("sync", lambda e: None, waits=[("d_y0", 16 * NST)])
        with nc.Block() as blk:
            blk.sync(lambda e: pl.replay("sync", e))
            blk.gpsimd(lambda e: pl.replay("gpsimd", e))
            blk.tensor(lambda e: pl.replay("tensor", e))
            blk.scalar(lambda e: pl.replay("scalar", e))
            blk.vector(lambda e: pl.replay("vector", e))
        stck.close()
        return nc, sbuf_used

    W("gpsimd", lambda e: e.dma_start(out=s2_wo[:], in_=wo_d.rearrange("(k p) m -> p k m", p=128)),
      waits=[("d_w", 64), ("pe", vmm_pe[NST - 1]), ("d_vt", 16 * NST)], inc="d_w", amt=16)
    wo_loaded = pl.val["d_w"]

    def mid_fn(e):
        last = None
        for hh in range(H):
            e.tensor_copy(s2_kvb[:, hh, :], s_kvsb[:, hh, 0:128])
            last = e.tensor_scalar(s2_fkr[:, hh, :], s_ones[:], s_kvsb[:, hh, 128:129], None, ALU.mult)
        return last
    W("vector", mid_fn, waits=[("dve", kve_done[NST - 1])], inc="dve")
    mid_done = pl.val["dve"]

    sc_done, exp_done, msk_done, tele_done = {}, {}, {}, {}
    norm_done, norm_half, ch_load = {}, {}, {}
    q2l, cv2l = [0] * NST, [0] * NST
    p2l = [0] * NT
    h_done = [0] * NST
    wo_done = [0] * NST
    out_done = [0] * NT

    def att_i(r, hh):
        return r * 8 + hh

    def chload_fns(r):
        st_, rm = r // 4, r % 4

        def fk_(e):
            return e.dma_start(out=s2_kr[:, r % 18, :, :], in_=k_sp[st_, :, :, 128 * rm:128 * rm + 128])

        def fv_(e):
            return e.dma_start(out=s2_vr[:, r % 18, :, :], in_=vt_sp[st_, :, rm, :, :])
        return fk_, fv_

    ch_load[0] = (("d_chk", pre["ch0k"]), ("d_chv", pre["ch0v"]))

    for st in range(NST):
        sl = st % 2
        q2s = "d_q2a" if sl == 0 else "d_q2b"
        cv2s = "d_cv2a" if sl == 0 else "d_cv2b"
        if st == 0:
            q2l[0] = pre["q2"]
        else:
            q2w = [("d_q", 16 * (st + 1))]
            if st >= 2:
                q2w.append(("dve", h_done[st - 2]))
            else:
                q2w.append(("dve", lnf_done[NT - 1]))
                q2w.append(("pe", vmm_pe[NST - 1]))
            W("sync", (lambda st, sl: lambda e: e.dma_start(out=s2_q[:, sl, :, :], in_=q_sp[st]))(st, sl),
              waits=q2w, inc=q2s, amt=16)
            q2l[st] = (q2s, pl.val[q2s])
        cv2w = [("d_cv", 16 * (st + 1))]
        if st >= 2:
            cv2w.append(("dve", h_done[st - 2]))
        else:
            cv2w.append(("dve", pas_last[NST - 1]))
            cv2w.append(("pe", qkv_pe[NST - 1]))
        W("sync",
          (lambda st, sl: lambda e: e.dma_start(out=s2_cv[:, sl, :, :], in_=cv_sp[st]))(st, sl),
          waits=cv2w, inc=cv2s, amt=16)
        cv2l[st] = (cv2s, pl.val[cv2s])

        for rt in range(4):
            r = 4 * st + rt
            p2s = "d_p2a" if rt % 2 == 0 else "d_p2b"
            wts = [("d_p", sweep1_p)]
            if st == 0 and rt < 2:
                wts.append(("pe", kv_pe[NST - 1]))
                wts.append(("d_vt", 16 * NST))
                wts.append(("d_cv", 16 * NST))
            if r >= 4:
                wts.append((("d_y0", "d_y1", "d_y2", "d_y3")[rt], 16 * st))
            if rt >= 2:
                wts.append((p2s, pl.val[p2s]))
            W("sync", (lambda st, rt: lambda e: e.dma_start(
                out=s2_p[:, rt, :], in_=p_sp[st, :, rt, :]))(st, rt),
                waits=wts, inc=p2s, amt=16)
            p2l[r] = (p2s, pl.val[p2s])

        fqf = s2_fq[:].rearrange("p h t -> p (h t)")
        fqsf = s2_fqs[:].rearrange("p h t -> p (h t)")
        qf = s2_q[:, sl, :, :].rearrange("p h t -> p (h t)")
        f1w = [q2l[st]]
        if st >= 1:
            f1w.append(("dve", h_done[st - 1]))
        else:
            f1w.append(("dve", pas_last[NST - 1]))
            f1w.append(("pe", trans_pe[NST - 1]))
        W("vector", (lambda qf: lambda e: e.tensor_scalar(fqsf, qf, 0.0, None, ALU.min))(qf),
          waits=f1w, inc="dve")
        f1 = pl.val["dve"]
        W("scalar", lambda e: e.activation(fqf, fqsf, AF.Exp), waits=[("dve", f1)], inc="act")
        f2 = pl.val["act"]

        def fq3_fn(e, qf=qf):
            cinc(e.tensor_scalar(fqsf, qf, 0.0, None, ALU.max))
            cwt(e)
            return e.tensor_tensor(fqf, fqf, fqsf, ALU.add)
        W("vector", fq3_fn, waits=[("act", f2)], inc="dve")
        fq_done = pl.val["dve"]

        pend = [None]

        def emit_te(p):
            r2, rt2, hh2, cls2, n2, eslot2, i2 = p

            def te_fn(e, hh=hh2, cls=cls2, n=n2, eslot=eslot2):
                tb = 4 + hh // 4
                rb = 6 + hh // 4
                tcol = 128 * (hh % 4)
                last = None
                for ci, c in enumerate(cls):
                    last = e.matmul(bank(tb, tcol, tcol + 128), s2_vr[:, c % 18, hh, :],
                                    s2_er[:, eslot, 128 * ci:128 * (ci + 1)],
                                    start=(ci == 0), stop=(ci == n - 1), skip_group_check=True)
                for ci in range(n):
                    last = e.matmul(bank(rb, tcol, tcol + 128), s_ones[:],
                                    s2_er[:, eslot, 128 * ci:128 * (ci + 1)],
                                    start=(ci == 0), stop=(ci == n - 1), skip_group_check=True)
                return last
            wts = [("act", exp_done[i2]), ch_load[r2][0], ch_load[r2][1]]
            if r2 >= 1:
                wts.append(("dve", norm_half.get((r2 - 1, hh2 // 4), 0)))
            if rt2 == 0 and hh2 == 0 and st > 0:
                wts.append(("dve", out_done[4 * st - 1]))
            W("tensor", te_fn, waits=wts, inc="pe")
            tele_done[i2] = pl.val["pe"]

            if hh2 == 3 or hh2 == 7:
                half = hh2 // 4

                def nr_fn(e, rt=rt2, half=half):
                    cinc(e.reciprocal(s2_rr[:, 512 * half:512 * half + 512],
                                      PS[:, 512 * (6 + half): 512 * (6 + half) + 512]))
                    cwt(e)
                    last = None
                    for hh3 in range(4 * half, 4 * half + 4):
                        last = e.tensor_tensor(s2_h[:, hh3, 128 * rt:128 * rt + 128],
                                               bank(4 + half, 128 * (hh3 % 4), 128 * (hh3 % 4) + 128),
                                               s2_rr[:, 128 * hh3:128 * hh3 + 128], ALU.mult)
                    return last
                W("vector", nr_fn, waits=[("pe", tele_done[i2])], inc="dve")
                norm_half[(r2, half)] = pl.val["dve"]
            if hh2 == 7:
                norm_done[r2] = norm_half[(r2, 1)]

        for rt in range(4):
            r = 4 * st + rt
            cls = chunks_of(r)
            n = len(cls)
            for hh in range(H):
                i = att_i(r, hh)
                sslot = i % 2
                eslot = i % 2

                def sc_fn(e, r=r, hh=hh, cls=cls, sslot=sslot, sl=sl):
                    last = None
                    for ci, c in enumerate(cls):
                        b0 = 2 * sslot + ci // 4
                        blk = ci % 4
                        dst = bank(b0, 128 * blk, 128 * blk + 128)
                        last = e.matmul(dst, s2_kr[:, c % 18, hh, :],
                                        s2_q[:, sl, hh, 128 * (r % 4):128 * (r % 4) + 128],
                                        start=True, stop=False, skip_group_check=True)
                        last = e.matmul(dst, s2_mlog[:, ci, :], s_id[:],
                                        start=False, stop=True, skip_group_check=True)
                    return last
                wts = [ch_load[r][0], ch_load[r][1], q2l[st]]
                if i >= 2:
                    wts.append(("act", exp_done[i - 2]))
                if hh < 2:
                    if rt == 0 and st > 0:
                        wts.append(("dve", h_done[st - 1]))
                        wts.append(("act", wo_done[st - 1]))
                W("tensor", sc_fn, waits=wts, inc="pe")
                sc_done[i] = pl.val["pe"]

                def ex_fn(e, n=n, sslot=sslot, eslot=eslot):
                    src = PS[:, 512 * (2 * sslot): 512 * (2 * sslot) + 128 * n]
                    return e.activation(s2_er[:, eslot, 0:128 * n], src, AF.Exp,
                                        bias=s_con[:, 0:1], scale=s_con[:, 1:2])
                wts = [("pe", sc_done[i])]
                if i >= 2:
                    wts.append(("pe", tele_done[i - 2]))
                W("scalar", ex_fn, waits=wts, inc="act")
                exp_done[i] = pl.val["act"]

                if pend[0] is not None:
                    emit_te(pend[0])
                pend[0] = (r, rt, hh, cls, n, eslot, i)

                if hh == 0 and r + 1 < NT:
                    fkl, fvl = chload_fns(r + 1)
                    stn = (r + 1) // 4
                    W("sync", fkl, waits=[("d_k", 16 * (stn + 1)),
                                          ("pe", sc_done[i]),
                                          ("pe", tele_done.get(att_i(r - 1, 7), 0))],
                      inc="d_chk", amt=16)
                    W("sync", fvl, waits=[("d_vt", 16 * (stn + 1)), ("pe", sc_done[i])],
                      inc="d_chv", amt=16)
                    ch_load[r + 1] = (("d_chk", pl.val["d_chk"]), ("d_chv", pl.val["d_chv"]))
        emit_te(pend[0])
        pend[0] = None

        hf = s2_h[:].rearrange("p h t -> p (h t)")
        W("vector", (lambda sl: lambda e: e.tensor_tensor(
            hf, hf, s2_cv[:, sl, :, :].rearrange("p h t -> p (h t)"), ALU.add))(sl),
            waits=[cv2l[st], ("dve", norm_done[4 * st + 3])], inc="dve")
        cadd_done = pl.val["dve"]

        lown = {}
        zev = {}
        zdone = [0, 0]
        for hp in range(4):
            h0, h1 = 2 * hp, 2 * hp + 1

            def z_fn(e, h0=h0, h1=h1):
                e.matmul(bank(0, 0, 512), s2_fkr[:, h0, :], s2_fq[:, h0, :],
                         start=True, stop=True, skip_group_check=True)
                return e.matmul(bank(1, 0, 512), s2_fkr[:, h1, :], s2_fq[:, h1, :],
                                start=True, stop=True, skip_group_check=True)
            wts = [("dve", fq_done), ("pe", tele_done[att_i(4 * st + 3, 7)])]
            if st == 0 and hp == 0:
                wts.append(("d_vt", 16 * NST))
            if hp >= 1:
                wts.append(("act", zev[hp - 1]))
            W("tensor", z_fn, waits=wts, inc="pe")
            zmm = pl.val["pe"]

            W("scalar", (lambda h0: lambda e: e.copy(s2_zb[:, h0, :], bank(0, 0, 512)))(h0),
              waits=[("pe", zmm)], inc="act")
            W("scalar", (lambda h1: lambda e: e.copy(s2_zb[:, h1, :], bank(1, 0, 512)))(h1),
              waits=[("pe", zmm)], inc="act")
            zev[hp] = pl.val["act"]

            if hp % 2 == 1:
                half = hp // 2
                zbf = s2_zb[:, 4 * half:4 * half + 4, :].rearrange("p s t -> p (s t)")
                W("scalar", (lambda zbf: lambda e: e.activation(zbf, zbf, AF.Ln, bias=s_con[:, 2:3]))(zbf),
                  waits=[("act", zev[hp])], inc="act")
                lnv = pl.val["act"]
                W("scalar", (lambda zbf: lambda e: e.activation(zbf, zbf, AF.Exp, scale=s_con[:, 4:5]))(zbf),
                  waits=[("act", lnv)], inc="act")
                zdone[half] = pl.val["act"]

        LBNK = (4, 5, 0, 1)
        for hh in range(H):
            zb_ = LBNK[hh % 4]

            def lmm_fn(e, hh=hh, zb_=zb_):
                return e.matmul(bank(zb_, 0, 512), s2_kvb[:, hh, :], s2_fq[:, hh, :],
                                start=True, stop=True, skip_group_check=True)
            lw = [("dve", lown[hh - 4]) if hh >= 4 else ("dve", norm_done[4 * st + 3])]
            if 2 <= hh < 4:
                lw.append(("act", zev[3]))
            W("tensor", lmm_fn, waits=lw, inc="pe")
            lmm = pl.val["pe"]

            def lno_fn(e, hh=hh, zb_=zb_):
                cinc(e.tensor_tensor(s2_scr[:], bank(zb_, 0, 512), s2_zb[:, hh, :], ALU.mult))
                cwt(e)
                return e.tensor_tensor(s2_h[:, hh, :], s2_h[:, hh, :], s2_scr[:], ALU.add)
            W("vector", lno_fn, waits=[("pe", lmm), ("act", zdone[hh // 4]), ("dve", cadd_done),
                                       ("dve", lown.get(hh - 1, 0))], inc="dve")
            lown[hh] = pl.val["dve"]
        h_done[st] = lown[7]

        woe = {}
        for mp in range(4):
            m0, m1 = 2 * mp, 2 * mp + 1

            lw = tuple(lown[kc] for kc in range(8))

            def wmm_fn(e, m0=m0, m1=m1, lw=lw):
                last = None
                for kc in range(8):
                    e.wait_ge(pl.sem["dve"], lw[kc])
                    e.matmul(bank(2, 0, 512), s2_wo[:, kc, 128 * m0:128 * (m0 + 1)],
                             s2_h[:, kc, :], start=(kc == 0), stop=(kc == 7),
                             skip_group_check=True)
                    last = e.matmul(bank(3, 0, 512), s2_wo[:, kc, 128 * m1:128 * (m1 + 1)],
                                    s2_h[:, kc, :], start=(kc == 0), stop=(kc == 7),
                                    skip_group_check=True)
                return last
            wts = [("d_w", wo_loaded)]
            if mp >= 1:
                wts.append(("act", woe[m1 - 2]))
            W("tensor", wmm_fn, waits=wts, inc="pe")
            wmm = pl.val["pe"]
            W("scalar", (lambda m0: lambda e: e.copy(s2_wos[:, m0, :], bank(2, 0, 512)))(m0),
              waits=[("pe", wmm)], inc="act")
            W("scalar", (lambda m1: lambda e: e.copy(s2_wos[:, m1, :], bank(3, 0, 512)))(m1),
              waits=[("pe", wmm)], inc="act")
            woe[m1] = pl.val["act"]
        wo_done[st] = woe[7]

        for rt in range(4):
            r = 4 * st + rt
            oslot = rt % 2

            def oT_fn(e, rt=rt, oslot=oslot):
                last = None
                for m in range(8):
                    last = e.transpose(
                        PSb[:, 1024 * (4 + oslot) + 128 * m: 1024 * (4 + oslot) + 128 * m + 128],
                        s2_wos[:, m, 128 * rt:128 * rt + 128], s_id[:])
                return last
            wts = [("act", wo_done[st])]
            if r >= 2:
                wts.append(("dve", out_done[r - 2]))
            if rt == 0:
                wts.append(("dve", norm_done[4 * st + 3]))
            W("tensor", oT_fn, waits=wts, inc="pe")
            otd = pl.val["pe"]

            def oadd_fn(e, rt=rt, oslot=oslot):
                return e.tensor_tensor(s2_p[:, rt, :], s2_p[:, rt, :],
                                       PSb[:, 1024 * (4 + oslot): 1024 * (4 + oslot) + 1024], ALU.add)
            W("vector", oadd_fn, waits=[("pe", otd), p2l[r]], inc="dve")
            out_done[r] = pl.val["dve"]

            W("sync", (lambda r, rt: lambda e: e.dma_start(
                out=y_d[128 * r:128 * (r + 1), :], in_=s2_p[:, rt, :]))(r, rt),
                waits=[("dve", out_done[r])], inc=("d_y0", "d_y1", "d_y2", "d_y3")[rt], amt=16)

    W("sync", lambda e: None, waits=[("d_y0", 16 * NST), ("d_y1", 16 * NST),
                                     ("d_y2", 16 * NST), ("d_y3", 16 * NST)])

    with nc.Block() as blk:
        blk.sync(lambda e: pl.replay("sync", e))
        blk.gpsimd(lambda e: pl.replay("gpsimd", e))
        blk.tensor(lambda e: pl.replay("tensor", e))
        blk.scalar(lambda e: pl.replay("scalar", e))
        blk.vector(lambda e: pl.replay("vector", e))

    stck.close()
    return nc, sbuf_used


_NC_CACHE = {}


def kernel(L0_attn_x, L0_attn_Wqkv, L0_attn_Wo, L0_attn_conv_w,
           L0_ffn_W_r, L0_ffn_W_k, L0_ffn_W_v, L0_ffn_mu_r, L0_ffn_mu_k):
    x = np.ascontiguousarray(np.asarray(L0_attn_x, dtype=np.float32))
    wqkv = np.ascontiguousarray(np.asarray(L0_attn_Wqkv, dtype=np.float32))
    wo = np.ascontiguousarray(np.asarray(L0_attn_Wo, dtype=np.float32))
    cw = np.asarray(L0_attn_conv_w, dtype=np.float32)
    wrt = np.ascontiguousarray(np.asarray(L0_ffn_W_r, dtype=np.float32).T)
    wkt = np.ascontiguousarray(np.asarray(L0_ffn_W_k, dtype=np.float32).T)
    wvt = np.ascontiguousarray(np.asarray(L0_ffn_W_v, dtype=np.float32).T)
    mur = np.ascontiguousarray(np.asarray(L0_ffn_mu_r, dtype=np.float32).reshape(H, 128).T)
    muk = np.ascontiguousarray(np.asarray(L0_ffn_mu_k, dtype=np.float32).reshape(H, 128).T)
    cwl = np.ascontiguousarray(cw.reshape(H, 128, 7).transpose(1, 0, 2))

    if "nc" not in _NC_CACHE:
        _NC_CACHE["nc"] = build_nc()
    nc, _ = _NC_CACHE["nc"]

    base = {"wqkv": wqkv, "wo": wo, "wrt": wrt, "wkt": wkt, "wvt": wvt,
            "cw": cwl, "mur": mur, "muk": muk}
    in_maps = [dict(base, x=np.ascontiguousarray(x[b])) for b in range(8)]
    res = run_bass_kernel_spmd(nc, in_maps, core_ids=list(range(8)))
    return np.stack([res.results[b]["y"] for b in range(8)], axis=0)


if __name__ == "__main__":
    nc, used = build_nc()
    print("sbuf bytes/partition used:", used, " room-224k:", 224 * 1024 - used)
    print("instructions:", nc.n_instructions())
